# version 24
# baseline (speedup 1.0000x reference)
import contextlib
import numpy as np

import concourse.bass as bass
import concourse.mybir as mybir
import bass_rust
from concourse.bass_utils import run_bass_kernel_spmd

F32 = mybir.dt.float32
BF16 = mybir.dt.bfloat16
AF = mybir.ActivationFunctionType
OP = mybir.AluOpType
AX = mybir.AxisListType

T = 2048
C = 1024
NH = 16
NE = 8
DE = 512
NT = T // 128
NQ = T // 512
NC = C // 128
EPS = 1e-5

ASSEMBLY = "opt"
RSTAGE = 9
DEBUG_PHASE = None


class Plan:
    def __init__(self):
        self.ops = {e: [] for e in ("tensor", "vector", "scalar", "sync", "gpsimd")}

    def add(self, eng, fn, waits=()):
        self.ops[eng].append((list(waits), fn))


def emit_block(nc, plan, name=None):
    active = [e for e in plan.ops if plan.ops[e]]
    if not active:
        return
    with nc.Block(name) as block:
        for eng_name in active:
            ops = plan.ops[eng_name]

            def body(eng, ops=ops):
                for waits, fn in ops:
                    for s, v in waits:
                        eng.wait_ge(s, v() if callable(v) else v)
                    fn(eng)

            getattr(block, eng_name)(body)


def build_nc():
    nc = bass.Bass("TRN2", target_bir_lowering=False)

    x_d = nc.dram_tensor("x", [T, C], F32, kind="ExternalInput")
    wt_d = nc.dram_tensor("wt", [C, 3 * C], F32, kind="ExternalInput")
    bqk_d = nc.dram_tensor("bqk", [128, 16], F32, kind="ExternalInput")
    bv_d = nc.dram_tensor("bv", [1, C], F32, kind="ExternalInput")
    w1_d = nc.dram_tensor("w1", [NE, C, DE], F32, kind="ExternalInput")
    b1_d = nc.dram_tensor("b1", [128, NE, 4], F32, kind="ExternalInput")
    w2_d = nc.dram_tensor("w2", [NE, DE, C], F32, kind="ExternalInput")
    b2_d = nc.dram_tensor("b2", [NE, C], F32, kind="ExternalInput")
    cent_d = nc.dram_tensor("cent", [NE, C], F32, kind="ExternalInput")
    wrt_d = nc.dram_tensor("wrt", [C, NE], F32, kind="ExternalInput")
    out_d = nc.dram_tensor("out", [T, C], F32, kind="ExternalOutput")
    dbg_d = {}

    sem_names = [
        "s_t", "s_v", "s_s", "s_vg", "s_g",
        "d_cst", "d_gcst", "d_gdma",
        "d_x0", "d_x1", "d_x2", "d_x3",
        "d_xnT", "d_xn0", "d_xn1", "d_xn2", "d_xn3",
        "d_w10", "d_w11", "d_w20", "d_w21",
        "d_wt0", "d_wt1", "d_wt2", "d_wt3",
        "d_wv",
        "d_y0", "d_y1", "d_y2", "d_y3",
        "d_xf0", "d_xf1",
        "d_ob0", "d_ob1",
        "d_out",
    ]

    es = contextlib.ExitStack()
    with es:
        sem = {n: es.enter_context(nc.semaphore(n)) for n in sem_names}
        CNT = {n: 0 for n in sem_names}

        def mk(fn, semname=None, n=1):
            if semname is None:
                return fn, None
            prev = CNT[semname]
            CNT[semname] += n
            s = sem[semname]
            is_dma = semname.startswith("d_")

            def f(eng, fn=fn, s=s, n=n, prev=prev, is_dma=is_dma):
                if is_dma and prev > 0:
                    eng.wait_ge(s, prev)
                fn(eng).then_inc(s, n)
            return f, CNT[semname]

        def sb(name, shape, dt):
            return es.enter_context(nc.sbuf_tensor(name, shape, dt))

        def phase_fence(p):
            ww = [(sem[n], CNT[n]) for n in sem_names if n.startswith("d_") and CNT[n] > 0]
            for eng in ("tensor", "vector", "scalar", "sync", "gpsimd"):
                p.add(eng, lambda e: e.nop(), waits=list(ww))

        def W(*pairs):
            return [(sem[n], v) for n, v in pairs]

        iota_c = sb("iota_c", [128, 1], F32)
        iota_rb = sb("iota_rb", [128, 512], F32)
        mtri = sb("mtri", [128, 128], BF16)
        idm_bf = sb("idm_bf", [128, 128], BF16)
        idm_f = sb("idm_f", [16, 16], F32)
        ones_bf = sb("ones_bf", [1, 128], BF16)
        bqk_sb = sb("bqk_sb", [128, 16], F32)
        bqkq_sb = sb("bqkq_sb", [128, 8], F32)
        b1_sb = sb("b1_sb", [128, NE, 4], F32)
        bv_sb = sb("bv_sb", [1, C], BF16)
        b2_sb = sb("b2_sb", [NE, C], BF16)
        wcat = sb("wcat", [128, NC, 16], BF16)
        negm = sb("negm", [128, 16], F32)
        rstd_s = sb("rstd_s", [128, NT], F32)
        invn_s = sb("invn_s", [128, NT], F32)
        gates_tm = sb("gates_tm", [128, NT, NE], F32)
        gates_fm = sb("gates_fm", [NE, T], BF16)
        obk = sb("obk", [128, NC, 16], BF16)
        stp_c = sb("stp_c", [128, 1], F32)
        colv = sb("colv", [128, 1], F32)
        rec_sb = sb("rec_sb", [128, 64], F32)
        nmx = sb("nmx", [16, 16], F32)
        negcol = sb("negcol", [16, 1], F32)
        negrow = sb("negrow", [1, 16], F32)
        negrow_bf = sb("negrow_bf", [1, 16], BF16)

        acc = sb("acc", [128, NT, C], BF16)
        xn_fm = sb("xn_fm", [128, NC, T], BF16)

        wtr = sb("wtr", [128, 4, NC, 128], BF16)

        es_w = es.enter_context(contextlib.ExitStack())
        w1sb = es_w.enter_context(nc.sbuf_tensor("w1sb", [128, 2, NC, DE], BF16))
        w2sb = es_w.enter_context(nc.sbuf_tensor("w2sb", [128, 2, 4, C], BF16))
        es_b1 = es_w.enter_context(contextlib.ExitStack())
        cent_sb = es_b1.enter_context(nc.sbuf_tensor("cent_sb", [NE, C], F32))
        x_ring = es_b1.enter_context(nc.sbuf_tensor("x_ring", [128, 8, C], F32))
        xn_tm = es_b1.enter_context(nc.sbuf_tensor("xn_tm", [128, 8, C], BF16))

        def w1_load(ei):
            def f(e, ei=ei):
                return e.dma_start(
                    w1sb[:, ei % 2, :, :],
                    bass.AP(w1_d, ei * C * DE, [[DE, 128], [128 * DE, NC], [1, DE]]))
            return f

        def w2_load(ei):
            def f(e, ei=ei):
                return e.dma_start(
                    w2sb[:, ei % 2, :, :],
                    bass.AP(w2_d, ei * DE * C, [[C, 128], [128 * C, 4], [1, C]]))
            return f

        def wt_load(j):
            def f(e, j=j):
                return e.dma_start(
                    wtr[:, j % 4, :, :],
                    bass.AP(wt_d, j * 128, [[3 * C, 128], [128 * 3 * C, NC], [1, 128]]))
            return f

        def x_load(m):
            def f(e, m=m):
                return e.dma_start(
                    x_ring[:, m % 8, :],
                    bass.AP(x_d, m * 128 * C, [[C, 128], [1, C]]))
            return f

        p = Plan()
        p.add("gpsimd", mk(lambda e: e.iota(iota_c[:, :], [[1, 1]], channel_multiplier=1,
                                            allow_small_or_imprecise_dtypes=True), "d_gcst")[0])
        p.add("gpsimd", mk(lambda e: e.iota(iota_rb[:, :], [[1, 512]], channel_multiplier=0,
                                            allow_small_or_imprecise_dtypes=True), "d_gcst")[0])
        p.add("gpsimd", mk(lambda e: e.memset(acc[:, :, :], 0.0), "d_gcst")[0])
        p.add("gpsimd", mk(lambda e: e.memset(ones_bf[:, :], 1.0), "d_gcst")[0])
        gsetup = CNT["d_gcst"]

        p.add("vector", mk(lambda e: e.tensor_scalar(
            mtri[:, :], iota_rb[:, 0:128], iota_c[:, :], None, OP.is_ge), "s_v")[0],
            waits=W(("d_gcst", gsetup)))
        p.add("vector", mk(lambda e: e.tensor_scalar(
            idm_bf[:, :], iota_rb[:, 0:128], iota_c[:, :], None, OP.is_equal), "s_v")[0])
        p.add("vector", mk(lambda e: e.tensor_scalar(
            idm_f[:, :], iota_rb[0:16, 0:16], iota_c[0:16, :], None, OP.is_equal), "s_v")[0])
        fn, gob = mk(lambda e: e.tensor_scalar(
            stp_c[:, :], iota_c[:, :], 63.5, None, OP.is_gt), "s_vg")
        p.add("vector", fn)
        for cc in range(NC):
            fn, gob = mk(lambda e, cc=cc: e.tensor_scalar(
                colv[:, :], stp_c[:, :], float(2 * cc), None, OP.add), "s_vg")
            p.add("vector", fn, waits=W(("s_vg", gob - 1)))
            fn, gob = mk(lambda e, cc=cc: e.tensor_scalar(
                obk[:, cc, :], iota_rb[:, 0:16], colv[:, :], None, OP.is_equal), "s_vg")
            p.add("vector", fn, waits=W(("s_vg", gob - 1)))
        obk_done = gob

        p.add("sync", mk(lambda e: e.dma_start(cent_sb[:, :], cent_d[:, :]), "d_cst", 16)[0])
        cent_done = CNT["d_cst"]
        p.add("sync", mk(lambda e: e.dma_start(bqk_sb[:, :], bqk_d[:, :]), "d_cst", 16)[0])
        p.add("sync", mk(lambda e: e.dma_start(b1_sb[:, :, :], b1_d[:, :, :]), "d_cst", 16)[0])
        cst_b0 = CNT["d_cst"]
        p.add("gpsimd", mk(lambda e: e.dma_start(bv_sb[:, :], bv_d[:, :]), "d_gdma", 16)[0])
        p.add("gpsimd", mk(lambda e: e.dma_start(b2_sb[:, :], b2_d[:, :]), "d_gdma", 16)[0])
        p.add("gpsimd", mk(lambda e: e.dma_start(
            wcat[:, :, 8:16],
            bass.AP(wrt_d, 0, [[8, 128], [1024, 8], [1, 8]])), "d_gdma", 16)[0])

        p.add("vector", mk(lambda e: e.tensor_scalar(
            bqkq_sb[:, :], bqk_sb[:, 0:8], 0.125, None, OP.mult), "s_v")[0],
            waits=W(("d_cst", cst_b0)))

        w1_done, w2_done, wt_done, x_done = {}, {}, {}, {}
        for ei in (0, 1):
            fn, w1_done[ei] = mk(w1_load(ei), f"d_w1{ei % 2}", 16)
            p.add("gpsimd", fn)
            fn, w2_done[ei] = mk(w2_load(ei), f"d_w2{ei % 2}", 16)
            p.add("gpsimd", fn)
        for j in range(4):
            fn, wt_done[j] = mk(wt_load(j), f"d_wt{j % 4}", 16)
            p.add("gpsimd", fn)
        for m in range(8):
            fn, x_done[m] = mk(x_load(m), f"d_x{m % 4}", 16)
            p.add("sync", fn)
        emit_block(nc, p, "b0")

        bn6 = es_b1.enter_context(nc.sbuf_tensor("bn6", [128, 4, 12], F32))
        mvv = es_b1.enter_context(nc.sbuf_tensor("mvv", [128, NT, 2], F32))
        t1g = es_b1.enter_context(nc.sbuf_tensor("t1g", [128, 4], F32))
        t2b = es_b1.enter_context(nc.sbuf_tensor("t2b", [128, NT], F32))
        t3b = es_b1.enter_context(nc.sbuf_tensor("t3b", [128, NT], F32))
        nsq = es_b1.enter_context(nc.sbuf_tensor("nsq", [128, NT], F32))
        with contextlib.ExitStack() as es2:
            def sb2(name, shape, dt):
                return es2.enter_context(nc.sbuf_tensor(name, shape, dt))
            cn_bf = sb2("cn_bf", [NE, C], BF16)
            csq = sb2("csq", [NE, C], F32)
            cno = sb2("cno", [NE, 1], F32)
            gates_bf = sb2("gates_bf", [128, NT, NE], BF16)
            lg8 = sb2("lg8", [128, NE], F32)
            lgm = sb2("lgm", [128, NE], F32)
            lgs = sb2("lgs", [128, NT, NE], F32)
            lgk = sb2("lgk", [128, NT, NE], F32)
            m1c = sb2("m1c", [128, 1], F32)
            m2c = sb2("m2c", [128, 1], F32)
            ssum = sb2("ssum", [128, 1], F32)
            hsb = sb2("hsb", [128, 4, T], BF16)
            tps = es2.enter_context(nc.psum_tensor("tps", [128, 1024], BF16))
            hps = [es2.enter_context(nc.psum_tensor(f"hps{i}", [128, 512], F32)) for i in range(2)]
            ops_ = [es2.enter_context(nc.psum_tensor(f"ops{i}", [128, 512], F32)) for i in range(2)]
            b2ps = [es2.enter_context(nc.psum_tensor(f"b2ps{i}", [128, 512], F32)) for i in range(2)]
            lgps = b2ps

            SEG = {}
            class SegP:
                def __init__(self, name):
                    self.ops = {e: [] for e in ("tensor", "vector", "scalar", "sync", "gpsimd")}
                    SEG[name] = self
                def add(self, eng, fn, waits=()):
                    self.ops[eng].append((list(waits), fn))
            s_cn = SegP("cn")
            s_ln = [SegP(f"ln{m}") for m in range(NT)]
            s_lnB = [SegP(f"lnB{m}") for m in range(NT)]
            s_rA = [SegP(f"rA{m}") for m in range(NT)]
            s_rB = [SegP(f"rB{m}") for m in range(NT)]
            s_rC = [SegP(f"rC{m}") for m in range(NT)]
            s_ffn0 = SegP("ffn0")
            s_ffn1 = SegP("ffn1")

            p = s_cn
            cst_now = cent_done
            fn, g0 = mk(lambda e: e.tensor_tensor(csq[:, :], cent_sb[:, :], cent_sb[:, :], OP.mult), "s_vg")
            p.add("vector", fn, waits=W(("d_cst", cst_now)))
            fn, g1 = mk(lambda e: e.reduce_sum(cno[:, :], csq[:, :], AX.X), "s_vg")
            p.add("vector", fn, waits=W(("s_vg", g0)))
            fn, c_cno = mk(lambda e: e.nop(), "s_v")
            p.add("vector", fn, waits=W(("s_vg", g1)))
            fn, c_csqr = mk(lambda e: e.activation(cno[:, :], cno[:, :], AF.Sqrt), "s_s")
            p.add("scalar", fn, waits=W(("s_v", c_cno)))
            fn, g2 = mk(lambda e: e.tensor_scalar(cno[:, :], cno[:, :], 1e-12, None, OP.max), "s_vg")
            p.add("vector", fn, waits=W(("s_s", c_csqr)))
            fn, g3 = mk(lambda e: e.reciprocal(cno[:, :], cno[:, :]), "s_vg")
            p.add("vector", fn, waits=W(("s_vg", g2)))
            fn, c_cn = mk(lambda e: e.tensor_scalar(
                cn_bf[:, :], cent_sb[:, :], cno[:, :], None, OP.mult), "s_v")
            p.add("vector", fn, waits=W(("s_vg", g3)))
            ev_cn = {}
            for c in range(NC):
                w = W(("s_v", c_cn))
                if c > 0:
                    w += W(("s_s", ev_cn[c - 1]))
                fn, c_tp = mk(lambda e, c=c: e.transpose(
                    tps[:, 0:8], cn_bf[0:8, 128 * c:128 * (c + 1)], idm_bf[0:8, 0:8]), "s_t")
                p.add("tensor", fn, waits=w)
                fn, ev_cn[c] = mk(lambda e, c=c: e.activation(
                    wcat[:, c, 0:8], tps[:, 0:8], AF.Identity, bias=0.0, scale=1.0), "s_s")
                p.add("scalar", fn, waits=W(("s_t", c_tp)))

            xn_done = {}
            xn_t_done = {}
            c_rstdg, c_nsqg = {}, {}
            ga_fence = {}
            gb_fence = {}

            def var_ap(g):
                return bass.AP(mvv, 8 * g + 1, [[NT * 2, 128], [2, 4]])

            def plan_lnA(g):
                p = s_ln[g]
                gaggr = None
                for i in range(4):
                    m = 4 * g + i
                    w0 = W((f"d_x{m % 4}", x_done[m]))
                    if i == 0 and g >= 1:
                        w0 += W(("s_vg", ga_fence[g - 1]))
                    p.add("vector", mk(lambda e, i=i, m=m: e.bn_stats(
                        bn6[:, i, 0:6], x_ring[:, m % 8, 0:512]))[0], waits=w0)
                    fn, gs = mk(lambda e, i=i, m=m: e.bn_stats(
                        bn6[:, i, 6:12], x_ring[:, m % 8, 512:1024]), "s_vg")
                    p.add("vector", fn)
                for i in range(4):
                    m = 4 * g + i
                    fn, gaggr = mk(lambda e, i=i, m=m: e.bn_aggr(
                        mvv[:, m, :], bn6[:, i, :]), "s_vg")
                    p.add("vector", fn, waits=W(("s_vg", gs)) if i == 0 else ())
                fn, g1 = mk(lambda e, g=g: e.tensor_scalar(
                    t1g[:, :], var_ap(g), EPS, None, OP.add), "s_vg")
                wg1 = W(("s_vg", gaggr))
                if g >= 2:
                    wg1 += W(("s_vg", gb_fence[g - 2]))
                p.add("vector", fn, waits=wg1)
                fn, c_rvar4 = mk(lambda e, g=g: e.reciprocal(
                    t2b[:, 4 * g:4 * g + 4], t1g[:, :]), "s_v")
                p.add("vector", fn, waits=W(("s_vg", g1)))
                fn, g2 = mk(lambda e, g=g: e.scalar_tensor_tensor(
                    t3b[:, 4 * g:4 * g + 4], var_ap(g), float(C), t2b[:, 4 * g:4 * g + 4],
                    OP.mult, OP.mult), "s_vg")
                p.add("vector", fn, waits=W(("s_v", c_rvar4)))
                ga_fence[g] = g2
                fn, c_t3 = mk(lambda e: e.nop(), "s_v")
                p.add("vector", fn, waits=W(("s_vg", g2)))
                fn, c_rstdg[g] = mk(lambda e, g=g: e.activation(
                    rstd_s[:, 4 * g:4 * g + 4], t2b[:, 4 * g:4 * g + 4], AF.Sqrt), "s_s")
                p.add("scalar", fn, waits=W(("s_v", c_rvar4)))
                fn, c_nsqg[g] = mk(lambda e, g=g: e.activation(
                    nsq[:, 4 * g:4 * g + 4], t3b[:, 4 * g:4 * g + 4], AF.Sqrt), "s_s")
                p.add("scalar", fn, waits=W(("s_v", c_t3)))

            def plan_lnB(g):
                p = s_lnB[g]
                for i in range(4):
                    m = 4 * g + i
                    wxn = W(("s_s", c_rstdg[g])) if i == 0 else []
                    if m >= 8:
                        wxn = list(wxn) + [(sem[f"d_xn{m % 4}"], xn_t_done[m - 8])]
                    fn, c_xn = mk(lambda e, m=m: e.tensor_scalar(
                        xn_tm[:, m % 8, :], x_ring[:, m % 8, :], mvv[:, m, 0:1], rstd_s[:, m:m + 1],
                        OP.subtract, OP.mult), "s_v")
                    p.add("vector", fn, waits=wxn)
                    xn_done[m] = c_xn
                    if m + 8 < NT:
                        fn, x_done[m + 8] = mk(x_load(m + 8), f"d_x{(m + 8) % 4}", 16)
                        p.add("sync", fn, waits=W(("s_v", c_xn)))
                    fn, xn_t_done[m] = mk(lambda e, m=m: e.dma_start(
                        xn_fm[:, :, 128 * m:128 * (m + 1)], xn_tm[:, m % 8, :], transpose=True),
                        f"d_xn{m % 4}", 16)
                    p.add("sync", fn, waits=W(("s_v", c_xn)))
                w3 = W(("s_s", c_nsqg[g]))
                if g >= 1:
                    w3 += W(("s_vg", gb_fence[g - 1]))
                w3 += W(("s_vg", ga_fence[min(g + 1, 3)]))
                fn, g3 = mk(lambda e, g=g: e.tensor_scalar(
                    t1g[:, :], nsq[:, 4 * g:4 * g + 4], 1e-12, None, OP.max), "s_vg")
                p.add("vector", fn, waits=w3)
                fn, g4 = mk(lambda e, g=g: e.reciprocal(
                    invn_s[:, 4 * g:4 * g + 4], t1g[:, :]), "s_vg")
                p.add("vector", fn, waits=W(("s_vg", g3)))
                gb_fence[g] = g4

            ln_order = []
            for seq in (("A", 0), ("A", 1), ("B", 0), ("A", 2), ("B", 1), ("A", 3), ("B", 2), ("B", 3)):
                kind, g = seq
                if kind == "A":
                    plan_lnA(g)
                    ln_order.append(s_ln[g])
                else:
                    plan_lnB(g)
                    ln_order.append(s_lnB[g])

            def xn_wait_tile(m):
                return [(sem[f"d_xn{m % 4}"], xn_t_done[m])]

            def xn_wait_tt(tt):
                ww = []
                for i in range(4):
                    ww += xn_wait_tile(4 * tt + i)
                return ww

            XNALL = [(sem[f"d_xn{i}"], xn_t_done[12 + i]) for i in range(4)]

            gcst_all = CNT["d_gdma"]
            rmm_cnt = {}
            lg_read = {}
            ra_fence = {}
            rb_fence = {}
            gfm_ev = {}
            c_gbf = {}
            c_sub = {}
            exp_rt = {}
            for m in range(NT if ASSEMBLY not in ("norouter", "lnonly") else 0):
                p = s_rA[m]
                bi = m % 2
                if RSTAGE < 1:
                    continue
                fn, g0 = mk(lambda e, m=m, bi=bi: e.tensor_scalar(
                    lg8[:, :], lgps[bi][:, 0:8], invn_s[:, m:m + 1], None, OP.mult), "s_vg")
                wg0 = [(sem["s_t"], lambda m=m: rmm_cnt[m])]
                if m >= 1:
                    wg0 += W(("s_vg", ra_fence[m - 1]))
                p.add("vector", fn, waits=wg0)
                fn, g1 = mk(lambda e, bi=bi: e.tensor_tensor(
                    lg8[:, :], lg8[:, :], lgps[bi][:, 8:16], OP.add), "s_vg")
                p.add("vector", fn, waits=W(("s_vg", g0)))
                fn, lg_read[m] = mk(lambda e: e.nop(), "s_v")
                p.add("vector", fn, waits=W(("s_vg", g1)))
                if RSTAGE < 2:
                    continue
                fn, g2 = mk(lambda e: e.reduce_max(m1c[:, :], lg8[:, :], AX.X), "s_vg")
                p.add("vector", fn)
                fn, g3 = mk(lambda e: e.tensor_scalar(
                    lgm[:, :], lg8[:, :], m1c[:, :], None, OP.is_ge), "s_vg")
                p.add("vector", fn, waits=W(("s_vg", g2)))
                fn, g4 = mk(lambda e: e.scalar_tensor_tensor(
                    lgm[:, :], lgm[:, :], -1e30, lg8[:, :], OP.mult, OP.add), "s_vg")
                p.add("vector", fn, waits=W(("s_vg", g3)))
                fn, g5 = mk(lambda e: e.reduce_max(m2c[:, :], lgm[:, :], AX.X), "s_vg")
                p.add("vector", fn, waits=W(("s_vg", g4)))
                fn, g6 = mk(lambda e, m=m: e.tensor_scalar(
                    lgs[:, m, :], lg8[:, :], m1c[:, :], None, OP.subtract), "s_vg")
                p.add("vector", fn, waits=W(("s_vg", g5)))
                fn, c_sub[m] = mk(lambda e: e.nop(), "s_v")
                p.add("vector", fn, waits=W(("s_vg", g6)))
                fn, g7 = mk(lambda e, m=m: e.tensor_scalar(
                    lgk[:, m, :], lg8[:, :], m2c[:, :], None, OP.is_ge), "s_vg")
                p.add("vector", fn)
                g7_rt = g7
                ra_fence[m] = g7

            ra_g7 = dict(ra_fence)
            for m in range(NT if (ASSEMBLY not in ("norouter", "lnonly") and RSTAGE >= 3) else 0):
                p = s_rB[m]
                fn, g8 = mk(lambda e, m=m: e.tensor_tensor(
                    lgk[:, m, :], lgk[:, m, :], lgs[:, m, :], OP.mult), "s_vg")
                wg8 = [(sem["s_s"], lambda m=m: exp_rt[m])] + W(("s_vg", ra_g7[m]))
                if m >= 1 and (m - 1) in rb_fence:
                    wg8 += W(("s_vg", rb_fence[m - 1]))
                p.add("vector", fn, waits=wg8)
                if RSTAGE < 3.2:
                    continue
                fn, g9 = mk(lambda e, m=m: e.reduce_sum(ssum[:, :], lgk[:, m, :], AX.X), "s_vg")
                p.add("vector", fn, waits=W(("s_vg", g8)))
                if RSTAGE < 3.4:
                    continue
                fn, g10 = mk(lambda e: e.reciprocal(ssum[:, :], ssum[:, :]), "s_vg")
                p.add("vector", fn, waits=W(("s_vg", g9)))
                if RSTAGE < 3.8:
                    continue
                fn, g11 = mk(lambda e, m=m: e.tensor_scalar(
                    gates_tm[:, m, :], lgk[:, m, :], ssum[:, :], None, OP.mult), "s_vg")
                p.add("vector", fn, waits=W(("s_vg", g10)))
                rb_fence[m] = g11
                fn, c_gbf[m] = mk(lambda e, m=m: e.tensor_copy(
                    gates_bf[:, m, :], gates_tm[:, m, :]), "s_v")
                p.add("vector", fn, waits=W(("s_vg", g11)))
            c_gel, c_stt = {}, {}
            c_hmm_all, c_omm_all = {}, {}
            ffn_experts = list(range(NE if ASSEMBLY not in ("noffn", "seqnoffn", "lnonly") else 0))
            def plan_expert(ei, p, mid=None):
                sl = ei % 2
                for g in range(16):
                    hc, tt = divmod(g, 4)
                    bi = g % 2
                    w = []
                    if g == 0:
                        w += W((f"d_w1{sl}", w1_done[ei]))
                    if ei == 0:
                        w += xn_wait_tt(tt)
                    if (ei, g - 2) in c_gel:
                        w += W(("s_s", c_gel[(ei, g - 2)]))
                    elif ei > 0 and g < 2:
                        w += W(("s_s", c_gel[(ei - 1, 14 + g)]))
                    for c in range(NC):
                        fn, c_mm = mk(lambda e, sl=sl, hc=hc, tt=tt, bi=bi, c=c: e.matmul(
                            hps[bi][:, :], w1sb[:, sl, c, 128 * hc:128 * (hc + 1)],
                            xn_fm[:, c, 512 * tt:512 * (tt + 1)],
                            start=(c == 0), stop=(c == NC - 1)), "s_t")
                        p.add("tensor", fn, waits=w if c == 0 else ())
                    if g == 15:
                        c_hmm_all[ei] = c_mm
                    w2w = W(("s_t", c_mm))
                    if ei >= 1 and g == 0:
                        w2w += W(("s_t", c_omm_all[ei - 1]))
                    fn, c_gel[(ei, g)] = mk(lambda e, ei=ei, hc=hc, tt=tt, bi=bi: e.activation(
                        hsb[:, hc, 512 * tt:512 * (tt + 1)], hps[bi][:, :], AF.Gelu,
                        bias=b1_sb[:, ei, hc:hc + 1], scale=1.0), "s_s")
                    p.add("scalar", fn, waits=w2w)
                if ei + 2 < NE:
                    fn, w1_done[ei + 2] = mk(w1_load(ei + 2), f"d_w1{ei % 2}", 16)
                    p.add("gpsimd", fn, waits=W(("s_t", c_hmm_all[ei])))
                if mid is not None:
                    mid(p)
                for og in range(32):
                    t2, ch = divmod(og, 2)
                    bi = og % 2
                    w = []
                    if og == 0:
                        w += W((f"d_w2{sl}", w2_done[ei]), ("s_s", c_gel[(ei, 15)]))
                    if (ei, og - 2) in c_stt:
                        w += W(("s_v", c_stt[(ei, og - 2)]))
                    elif ei > 0 and og < 2:
                        w += W(("s_v", c_stt[(ei - 1, 30 + og)]))
                    for hc in range(4):
                        fn, c_mm = mk(lambda e, t2=t2, ch=ch, bi=bi, hc=hc, sl=sl: e.matmul(
                            ops_[bi][:, :], hsb[:, hc, 128 * t2:128 * (t2 + 1)],
                            w2sb[:, sl, hc, 512 * ch:512 * (ch + 1)],
                            start=(hc == 0), stop=(hc == 3)), "s_t")
                        p.add("tensor", fn, waits=w if hc == 0 else ())
                    if og == 31:
                        c_omm_all[ei] = c_mm
                    fn, c_stt[(ei, og)] = mk(lambda e, ei=ei, t2=t2, ch=ch, bi=bi: e.scalar_tensor_tensor(
                        acc[:, t2, 512 * ch:512 * (ch + 1)], ops_[bi][:, :],
                        gates_tm[:, t2, ei:ei + 1], acc[:, t2, 512 * ch:512 * (ch + 1)],
                        OP.mult, OP.add), "s_v")
                    stt_w = W(("s_t", c_mm))
                    if ei == 0 and og == 0:
                        stt_w += W(("d_gcst", gsetup))
                    p.add("vector", fn, waits=stt_w)
                if ei + 2 < NE:
                    fn, w2_done[ei + 2] = mk(w2_load(ei + 2), f"d_w2{ei % 2}", 16)
                    p.add("gpsimd", fn, waits=W(("s_t", c_omm_all[ei])))
            def plan_router(p):
                for m in range(NT):
                    w = xn_wait_tile(m) + W(("s_s", ev_cn[NC - 1]), ("d_gdma", gcst_all))
                    if m >= 2:
                        w += W(("s_v", lg_read[m - 2]))
                    bi = m % 2
                    c_mm = None
                    for c in range(NC):
                        fn, c_mm = mk(lambda e, m=m, c=c, bi=bi: e.matmul(
                            lgps[bi][:, 0:16], xn_fm[:, c, 128 * m:128 * (m + 1)], wcat[:, c, :],
                            start=(c == 0), stop=(c == NC - 1)), "s_t")
                        p.add("tensor", fn, waits=w if c == 0 else ())
                    rmm_cnt[m] = c_mm
                    fn, exp_rt[m] = mk(lambda e, m=m: e.activation(
                        lgs[:, m, :], lgs[:, m, :], AF.Exp), "s_s")
                    p.add("scalar", fn, waits=W(("s_v", c_sub[m])))
            for ei in ffn_experts[:1]:
                plan_expert(ei, s_ffn0, mid=plan_router)
            c_gtp = {}
            if RSTAGE < 4:
                pass
            for m in range(NT if (ASSEMBLY not in ("norouter", "lnonly") and RSTAGE >= 4) else 0):
                p = s_rC[m]
                w = W(("s_v", c_gbf[m]))
                if m >= 1:
                    w += W(("s_v", gfm_ev[m - 1]))
                if m == 0:
                    w += W(("s_s", ev_cn[NC - 1]))
                fn, c_gtp[m] = mk(lambda e, m=m: e.transpose(
                    tps[0:8, 128:256], gates_bf[:, m, :], idm_bf[:, :]), "s_t")
                p.add("tensor", fn, waits=w)
                fn, gfm_ev[m] = mk(lambda e, m=m: e.tensor_copy(
                    gates_fm[:, 128 * m:128 * (m + 1)], tps[0:8, 128:256]), "s_v")
                p.add("vector", fn, waits=W(("s_t", c_gtp[m])))
            for ei in ffn_experts[1:]:
                plan_expert(ei, s_ffn1)
            p = s_ffn1
            c_b2ev = {}
            for og in range(32 if ASSEMBLY not in ("noffn", "seqnoffn", "norouter", "lnonly") else 0):
                t2, ch = divmod(og, 2)
                bi2 = og % 2
                w = []
                if og == 0:
                    w += W(("d_gdma", gcst_all), ("s_v", lg_read[NT - 1]),
                           ("s_v", gfm_ev[NT - 1]))
                if og >= 2:
                    w += W(("s_v", c_b2ev[og - 2]))
                fn, c_mm = mk(lambda e, t2=t2, ch=ch, bi2=bi2: e.matmul(
                    b2ps[bi2][:, :], gates_fm[0:8, 128 * t2:128 * (t2 + 1)],
                    b2_sb[0:8, 512 * ch:512 * (ch + 1)], start=True, stop=True), "s_t")
                p.add("tensor", fn, waits=w)
                fn, c_b2ev[og] = mk(lambda e, t2=t2, ch=ch, bi2=bi2: e.tensor_tensor(
                    acc[:, t2, 512 * ch:512 * (ch + 1)], acc[:, t2, 512 * ch:512 * (ch + 1)],
                    b2ps[bi2][:, :], OP.add), "s_v")
                p.add("vector", fn, waits=W(("s_t", c_mm)))

            p = Plan()
            def cat(eng, *segs):
                for s in segs:
                    p.ops[eng].extend(s.ops[eng])
            cat("sync", s_cn, *ln_order)
            cat("gpsimd", s_ffn0, s_ffn1)
            cat("tensor", s_cn, *s_rA, s_ffn0, *s_rC, s_ffn1)
            cat("vector", s_cn, *ln_order, *s_rA, *s_rB, s_ffn0, *s_rC, s_ffn1)
            cat("scalar", s_cn, *ln_order, *s_rA, s_ffn0, s_ffn1)
            emit_block(nc, p, "b123")
        es_b1.close()
        es_w.close()

        def finish_dbg(p):
            p.ops["sync"].append((W(("d_out", CNT["d_out"])), lambda e: e.nop()))
            emit_block(nc, p, "dbg")
            return nc, dbg_d

        if DEBUG_PHASE == "b123":
            dbg_d["dbg_acc"] = nc.dram_tensor("dbg_acc", [128, NT, C], BF16, kind="ExternalOutput")
            p = Plan()
            p.add("sync", mk(lambda e: e.dma_start(
                dbg_d["dbg_acc"][:, :, :], acc[:, :, :]), "d_out", 16)[0],
                waits=W(("s_v", CNT["s_v"]), ("s_s", CNT["s_s"]), ("s_t", CNT["s_t"])))
            return finish_dbg(p)

        es_qkv = es.enter_context(contextlib.ExitStack())
        q_fm = es_qkv.enter_context(nc.sbuf_tensor("q_fm", [128, NC, T], BF16))
        k_fm = es_qkv.enter_context(nc.sbuf_tensor("k_fm", [128, NC, T], BF16))
        v_sb = es_qkv.enter_context(nc.sbuf_tensor("v_sb", [128, NT, NH * 65], BF16))

        with contextlib.ExitStack() as es4:
            qps = [es4.enter_context(nc.psum_tensor(f"qps{i}", [128, 512], F32)) for i in range(4)]
            nps = es4.enter_context(nc.psum_tensor("nps", [128, 16], F32))
            sqb = es4.enter_context(nc.sbuf_tensor("sqb", [128, 2, T], BF16))

            p = Plan()
            phase_fence(p)
            p.add("gpsimd", mk(lambda e: e.memset(
                bass.AP(v_sb, 64, [[NT * NH * 65, 128], [NH * 65, NT], [65, NH]]), 1.0),
                "d_gcst")[0])
            vones = CNT["d_gcst"]

            c_qkev = {}
            gidx = 0
            for j in range(16):
                slot = j % 4
                for tt in range(4):
                    bi = gidx % 4
                    w = []
                    if tt == 0:
                        w += W((f"d_wt{slot}", wt_done[j]))
                        if j == 0:
                            w += XNALL
                    if gidx - 4 >= 0:
                        w += W(("s_s", c_qkev[gidx - 4]))
                    for c in range(NC):
                        fn, c_mm = mk(lambda e, slot=slot, c=c, tt=tt, bi=bi: e.matmul(
                            qps[bi][:, :], wtr[:, slot, c, :], xn_fm[:, c, 512 * tt:512 * (tt + 1)],
                            start=(c == 0), stop=(c == NC - 1)), "s_t")
                        p.add("tensor", fn, waits=w if c == 0 else ())
                    if j < 8:
                        fn, c_qkev[gidx] = mk(lambda e, j=j, tt=tt, bi=bi: e.activation(
                            q_fm[:, j, 512 * tt:512 * (tt + 1)], qps[bi][:, :], AF.Identity,
                            bias=bqkq_sb[:, j:j + 1], scale=0.125), "s_s")
                    else:
                        fn, c_qkev[gidx] = mk(lambda e, j=j, tt=tt, bi=bi: e.activation(
                            k_fm[:, j - 8, 512 * tt:512 * (tt + 1)], qps[bi][:, :], AF.Identity,
                            bias=bqk_sb[:, j:j + 1], scale=1.0), "s_s")
                    p.add("scalar", fn, waits=W(("s_t", c_mm)))
                    gidx += 1
                if j + 4 < 20:
                    fn, wt_done[j + 4] = mk(wt_load(j + 4), f"d_wt{(j + 4) % 4}", 16)
                    p.add("gpsimd", fn, waits=W(("s_t", c_mm)))
            c_vev = {}
            for vh in range(2):
                for t2 in range(NT):
                    vgi = 16 * vh + t2
                    bi = vgi % 4
                    w = []
                    if t2 == 0:
                        for jj in range(16 + 4 * vh, 20 + 4 * vh):
                            w += W((f"d_wt{jj % 4}", wt_done[jj]))
                        if vh == 0:
                            w += W(("d_gcst", vones))
                    if vgi - 4 >= 0:
                        w += W(("s_s", c_vev[vgi - 4]))
                    else:
                        w += W(("s_s", c_qkev[60 + vgi]))
                    for c in range(NC):
                        fn, c_mm = mk(lambda e, t2=t2, bi=bi, c=c: e.matmul(
                            qps[bi][:, :], xn_fm[:, c, 128 * t2:128 * (t2 + 1)],
                            bass.AP(wtr, c * 128, [[4 * NC * 128, 128], [NC * 128, 4], [1, 128]]),
                            start=(c == 0), stop=False), "s_t")
                        p.add("tensor", fn, waits=w if c == 0 else ())
                    fn, c_mm = mk(lambda e, vh=vh, bi=bi: e.matmul(
                        qps[bi][:, :], ones_bf[:, :], bv_sb[:, 512 * vh:512 * (vh + 1)],
                        start=False, stop=True), "s_t")
                    p.add("tensor", fn)
                    fn, c_vev[vgi] = mk(lambda e, t2=t2, vh=vh, bi=bi: e.activation(
                        bass.AP(v_sb, t2 * NH * 65 + vh * 8 * 65,
                                [[NT * NH * 65, 128], [65, 8], [1, 64]]),
                        qps[bi][:, :], AF.Identity, bias=0.0, scale=1.0), "s_s")
                    p.add("scalar", fn, waits=W(("s_t", c_mm)))
                if vh == 0:
                    for jj in range(20, 24):
                        fn, wt_done[jj] = mk(wt_load(jj), f"d_wt{jj % 4}", 16)
                        p.add("gpsimd", fn, waits=W(("s_t", c_mm)))
            norm_section = DEBUG_PHASE != "qkv0"
            if norm_section:
                c_sq = {}
                c_nmm = {}
                c_nred = {}
                last_vev = c_vev[31]
                for si, srcb in enumerate((q_fm, k_fm)):
                    for cc in range(NC):
                        sl2 = (si * NC + cc) % 2
                        w = []
                        if si == 0 and cc == 0:
                            w += W(("s_s", c_qkev[63]), ("s_vg", obk_done))
                        if (si * NC + cc) >= 2:
                            w += W(("s_t", c_nmm[si * NC + cc - 2]))
                        fn, c_sq[si * NC + cc] = mk(lambda e, srcb=srcb, cc=cc, sl2=sl2: e.tensor_tensor(
                            sqb[:, sl2, :], srcb[:, cc, :], srcb[:, cc, :], OP.mult), "s_v")
                        p.add("vector", fn, waits=w)
                        for tt in range(4):
                            wmm = []
                            if cc == 0 and tt == 0:
                                wmm += W(("s_v", c_sq[si * NC + cc]))
                                if si == 0:
                                    wmm += W(("s_s", last_vev))
                                else:
                                    wmm += W(("s_v", c_nred[3]))
                            elif tt == 0:
                                wmm += W(("s_v", c_sq[si * NC + cc]))
                            fn, c_mm2 = mk(lambda e, cc=cc, tt=tt, sl2=sl2: e.matmul(
                                qps[tt][0:16, :], obk[:, cc, :], sqb[:, sl2, 512 * tt:512 * (tt + 1)],
                                start=(cc == 0), stop=(cc == NC - 1)), "s_t")
                            p.add("tensor", fn, waits=wmm)
                        c_nmm[si * NC + cc] = c_mm2
                    gv0 = None
                    for tt in range(4):
                        fn, gv0 = mk(lambda e, si=si, tt=tt: e.reduce_max(
                            nmx[:, 8 * si + tt:8 * si + tt + 1], qps[tt][0:16, :], AX.X), "s_vg")
                        p.add("vector", fn, waits=W(("s_t", c_nmm[si * NC + NC - 1])))
                        c_nred[tt] = gv0 if si == 0 else c_nred[tt]
                    fn, gv0 = mk(lambda e, si=si: e.tensor_tensor(
                        nmx[:, 8 * si:8 * si + 1], nmx[:, 8 * si:8 * si + 1],
                        nmx[:, 8 * si + 1:8 * si + 2], OP.max), "s_vg")
                    p.add("vector", fn, waits=W(("s_vg", gv0 - 1)))
                    fn, gv0 = mk(lambda e, si=si: e.tensor_tensor(
                        nmx[:, 8 * si + 2:8 * si + 3], nmx[:, 8 * si + 2:8 * si + 3],
                        nmx[:, 8 * si + 3:8 * si + 4], OP.max), "s_vg")
                    p.add("vector", fn, waits=W(("s_vg", gv0 - 1)))
                    fn, gv0 = mk(lambda e, si=si: e.tensor_tensor(
                        nmx[:, 8 * si:8 * si + 1], nmx[:, 8 * si:8 * si + 1],
                        nmx[:, 8 * si + 2:8 * si + 3], OP.max), "s_vg")
                    p.add("vector", fn, waits=W(("s_vg", gv0 - 1)))
                    fn, c_vmax = mk(lambda e: e.nop(), "s_v")
                    p.add("vector", fn, waits=W(("s_vg", gv0)))
                    if si == 0:
                        c_nred[3] = c_vmax
                fn, c_sq1 = mk(lambda e: e.activation(
                    nmx[0:16, 4:5], nmx[0:16, 0:1], AF.Sqrt), "s_s")
                p.add("scalar", fn, waits=W(("s_v", c_vmax)))
                fn, c_sq2 = mk(lambda e: e.activation(
                    nmx[0:16, 5:6], nmx[0:16, 8:9], AF.Sqrt), "s_s")
                p.add("scalar", fn)
                fn, gn = mk(lambda e: e.tensor_tensor(
                    negcol[:, :], nmx[0:16, 4:5], nmx[0:16, 5:6], OP.mult), "s_vg")
                p.add("vector", fn, waits=W(("s_s", c_sq2)))
                fn, c_nc = mk(lambda e: e.tensor_scalar(
                    negcol[:, :], negcol[:, :], -1.0, None, OP.mult), "s_v")
                p.add("vector", fn, waits=W(("s_vg", gn)))
                fn, c_ntp = mk(lambda e: e.transpose(
                    nps[0:1, 0:16], negcol[:, :], idm_f[:, :]), "s_t")
                p.add("tensor", fn, waits=W(("s_v", c_nc)))
                fn, gn = mk(lambda e: e.tensor_copy(negrow[:, :], nps[0:1, 0:16]), "s_vg")
                p.add("vector", fn, waits=W(("s_t", c_ntp)))
                fn, c_nr = mk(lambda e: e.tensor_copy(negrow_bf[:, :], negrow[:, :]), "s_v")
                p.add("vector", fn, waits=W(("s_vg", gn)))
                fn, c_nmmb = mk(lambda e: e.matmul(
                    nps[:, :], ones_bf[:, :], negrow_bf[:, :], start=True, stop=True), "s_t")
                p.add("tensor", fn, waits=W(("s_v", c_nr)))
                fn, c_nev = mk(lambda e: e.tensor_copy(negm[:, :], nps[:, :]), "s_v")
                p.add("vector", fn, waits=W(("s_t", c_nmmb)))
            emit_block(nc, p, "b4_qkv")

        if DEBUG_PHASE in ("qkv", "qkv0"):
            dbg_d["dbg_q"] = nc.dram_tensor("dbg_q", [128, NC, T], BF16, kind="ExternalOutput")
            dbg_d["dbg_k"] = nc.dram_tensor("dbg_k", [128, NC, T], BF16, kind="ExternalOutput")
            dbg_d["dbg_v"] = nc.dram_tensor("dbg_v", [128, NT, NH * 65], BF16, kind="ExternalOutput")
            dbg_d["dbg_negm"] = nc.dram_tensor("dbg_negm", [128, 16], F32, kind="ExternalOutput")
            p = Plan()
            p.add("sync", mk(lambda e: e.dma_start(dbg_d["dbg_q"][:, :, :], q_fm[:, :, :]), "d_out", 16)[0])
            p.add("sync", mk(lambda e: e.dma_start(dbg_d["dbg_k"][:, :, :], k_fm[:, :, :]), "d_out", 16)[0])
            p.add("sync", mk(lambda e: e.dma_start(dbg_d["dbg_v"][:, :, :], v_sb[:, :, :]), "d_out", 16)[0])
            p.add("sync", mk(lambda e: e.dma_start(dbg_d["dbg_negm"][:, :], negm[:, :]), "d_out", 16)[0])
            return finish_dbg(p)

        with contextlib.ExitStack() as es5:
            pt = es5.enter_context(nc.sbuf_tensor("pt", [128, 4, 512], BF16))
            ystage = es5.enter_context(nc.sbuf_tensor("ystage", [80, 4, 512], BF16))
            yqt = es5.enter_context(nc.sbuf_tensor("yqt", [128, 4, NH * 80], BF16))
            xf = es5.enter_context(nc.sbuf_tensor("xf", [128, 2, C], F32))
            sps = [es5.enter_context(nc.psum_tensor(f"sps{i}", [128, 512], F32)) for i in range(4)]
            pvp = [es5.enter_context(nc.psum_tensor(f"pvp{i}", [65, 512], F32)) for i in range(4)]

            p = Plan()
            phase_fence(p)
            fn, c_ystz = mk(lambda e: e.memset(ystage[64:80, :, :], 0.0), "s_v")
            p.add("vector", fn)
            tj = 0
            exp_cnt, msk_cnt, pv_cnt = {}, {}, {}
            yev_cnt = {}
            yT_uses = {0: 0, 1: 0, 2: 0, 3: 0}
            acc_cnt = {}
            add_cnt = {}

            for qt in range(NQ):
                nkb = 4 * qt + 4
                for hp in range(8):
                    ghA = 16 * qt + 2 * hp
                    hA, hB = 2 * hp, 2 * hp + 1
                    gpar = (8 * qt + hp) % 2
                    pvA, pvB = pvp[2 * gpar], pvp[2 * gpar + 1]
                    sc_of = {}
                    for kb in range(nkb):
                        j = kb - 4 * qt
                        n0 = 128 * j if j >= 0 else 0
                        biA = sps[2 * (tj % 2)]
                        biB = sps[2 * (tj % 2) + 1]
                        w = []
                        if tj >= 2:
                            w += W(("s_s", exp_cnt[2 * (tj - 2) + 1]))
                        fn, _ = mk(lambda e, hp=hp, kb=kb, qt=qt, n0=n0, biA=biA: e.matmul(
                            biA[:, n0:512], k_fm[0:64, hp, 128 * kb:128 * (kb + 1)],
                            q_fm[0:64, hp, 512 * qt + n0:512 * (qt + 1)],
                            start=True, stop=True), "s_t")
                        p.add("tensor", fn, waits=w)
                        fn, cB = mk(lambda e, hp=hp, kb=kb, qt=qt, n0=n0, biB=biB: e.matmul(
                            biB[:, n0:512], k_fm[64:128, hp, 128 * kb:128 * (kb + 1)],
                            q_fm[64:128, hp, 512 * qt + n0:512 * (qt + 1)],
                            start=True, stop=True, tile_position=(64, 0)), "s_t")
                        p.add("tensor", fn)
                        sc_of[kb] = (tj, n0)
                        for which, (bank, head) in enumerate(((biA, hA), (biB, hB))):
                            ht = 2 * tj + which
                            slot = ht % 4
                            wS = W(("s_t", cB))
                            if ht - 4 >= 0:
                                wS += W(("s_t", pv_cnt[ht - 4]))
                            fn, exp_cnt[ht] = mk(lambda e, bank=bank, head=head, slot=slot, n0=n0: e.activation(
                                pt[:, slot, n0:512], bank[:, n0:512], AF.Exp,
                                bias=negm[:, head:head + 1], scale=1.0), "s_s")
                            p.add("scalar", fn, waits=wS)
                        if j >= 0:
                            for which in range(2):
                                ht = 2 * tj + which
                                slot = ht % 4
                                fn, msk_cnt[ht] = mk(lambda e, slot=slot, n0=n0: e.tensor_tensor(
                                    pt[:, slot, n0:n0 + 128], pt[:, slot, n0:n0 + 128],
                                    mtri[:, :], OP.mult), "s_v")
                                mw = W(("s_s", exp_cnt[ht]))
                                if tj == 0:
                                    mw += W(("d_gcst", gsetup))
                                p.add("vector", fn, waits=mw)
                        pv_list = []
                        if kb >= 1:
                            pv_list.append(kb - 1)
                        if kb == nkb - 1:
                            pv_list.append(kb)
                        for kpv in pv_list:
                            tjp, n0p = sc_of[kpv]
                            jp = kpv - 4 * qt
                            for which, (head, pvbank) in enumerate(((hA, pvA), (hB, pvB))):
                                htp = 2 * tjp + which
                                slot = htp % 4
                                wP = []
                                if jp >= 0:
                                    wP += W(("s_v", msk_cnt[htp]))
                                else:
                                    wP += W(("s_s", exp_cnt[htp]))
                                if kpv == 0 and ghA - 4 + which >= 0:
                                    wP += W(("s_v", yev_cnt.get(ghA - 4 + which, 0)))
                                fn, pv_cnt[htp] = mk(lambda e, head=head, pvbank=pvbank, kpv=kpv,
                                                     slot=slot, n0p=n0p, nkb=nkb: e.matmul(
                                    pvbank[:, n0p:512],
                                    v_sb[:, kpv, 65 * head:65 * head + 65],
                                    pt[:, slot, n0p:512],
                                    start=(kpv == 0), stop=(kpv == nkb - 1)), "s_t")
                                p.add("tensor", fn, waits=wP)
                        tj += 1
                    lastA = 2 * (tj - 1)
                    for which, (pvbank,) in enumerate(((pvA,), (pvB,))):
                        gh = ghA + which
                        head = 2 * hp + which
                        ysl = gh % 4
                        wv_ = W(("s_t", pv_cnt[lastA + which]))
                        if yT_uses[ysl] > 0:
                            wv_ += W((f"d_y{ysl}", 16 * yT_uses[ysl]))
                        fn, yev_cnt[gh] = mk(lambda e, ysl=ysl, pvbank=pvbank: e.tensor_copy(
                            ystage[0:65, ysl, :], pvbank[0:65, :]), "s_v")
                        p.add("vector", fn, waits=wv_)
                        wt2 = W(("s_v", yev_cnt[gh]))
                        if qt > 0:
                            wt2 += W(("s_v", acc_cnt[qt - 1]))
                        fn, _ = mk(lambda e, ysl=ysl, head=head: e.dma_start(
                            yqt[:, :, 80 * head:80 * (head + 1)], ystage[:, ysl, :],
                            transpose=True), f"d_y{ysl}", 16)
                        p.add("sync", fn, waits=wt2)
                        yT_uses[ysl] += 1
                wA = []
                for sl4 in range(4):
                    wA += W((f"d_y{sl4}", 16 * yT_uses[sl4]))
                g1 = None
                for u in range(4):
                    fn, g1 = mk(lambda e, u=u: e.reciprocal(
                        rec_sb[:, 16 * u:16 * (u + 1)],
                        bass.AP(yqt, u * NH * 80 + 64, [[4 * NH * 80, 128], [80, 16]])), "s_vg")
                    p.add("vector", fn, waits=wA if u == 0 else ())
                wA2 = W(("s_vg", g1))
                first = True
                for u in range(4):
                    for h in range(NH):
                        fn, c_add = mk(lambda e, qt=qt, u=u, h=h: e.scalar_tensor_tensor(
                            acc[:, 4 * qt + u, 64 * h:64 * (h + 1)],
                            yqt[:, u, 80 * h:80 * h + 64],
                            rec_sb[:, 16 * u + h:16 * u + h + 1],
                            acc[:, 4 * qt + u, 64 * h:64 * (h + 1)],
                            OP.mult, OP.add), "s_v")
                        p.add("vector", fn, waits=wA2 if first else ())
                        first = False
                acc_cnt[qt] = c_add
                for u in range(4):
                    m = 4 * qt + u
                    sl = m % 2
                    w = []
                    if m >= 2:
                        w += W((f"d_ob{sl}", 16 * (m // 2)))
                    fn, xf_done = mk(lambda e, m=m, sl=sl: e.dma_start(
                        xf[:, sl, :], bass.AP(x_d, m * 128 * C, [[C, 128], [1, C]])),
                        f"d_xf{sl}", 16)
                    p.add("gpsimd", fn, waits=w)
                    wv_ = W((f"d_xf{sl}", xf_done), ("s_v", acc_cnt[qt]))
                    fn, add_cnt[m] = mk(lambda e, m=m, sl=sl: e.tensor_add(
                        xf[:, sl, :], xf[:, sl, :], acc[:, m, :]), "s_g")
                    p.add("gpsimd", fn, waits=wv_)
                    fn, _ = mk(lambda e, m=m, sl=sl: e.dma_start(
                        bass.AP(out_d, m * 128 * C, [[C, 128], [1, C]]), xf[:, sl, :]),
                        f"d_ob{sl}", 16)
                    p.add("gpsimd", fn, waits=W(("s_g", add_cnt[m])))
            p.ops["gpsimd"].append(
                (W(("d_ob0", CNT["d_ob0"]), ("d_ob1", CNT["d_ob1"])), lambda e: e.nop()))
            emit_block(nc, p, "b5_attn")
        es_qkv.close()


    return nc, dbg_d


_NC_CACHE = {}


def get_nc():
    key = DEBUG_PHASE
    if key not in _NC_CACHE:
        _NC_CACHE[key] = build_nc()
    return _NC_CACHE[key]


def kernel(L0_attn_x, L0_attn_W, L0_attn_b, L0_ffn_w1, L0_ffn_b1, L0_ffn_w2,
           L0_ffn_b2, L0_ffn_centroids, L0_ffn_w_route):
    x = np.ascontiguousarray(np.asarray(L0_attn_x, dtype=np.float32))
    Wm = np.asarray(L0_attn_W, dtype=np.float32)
    b = np.asarray(L0_attn_b, dtype=np.float32)
    w1 = np.ascontiguousarray(np.asarray(L0_ffn_w1, dtype=np.float32))
    b1 = np.asarray(L0_ffn_b1, dtype=np.float32)
    w2 = np.ascontiguousarray(np.asarray(L0_ffn_w2, dtype=np.float32))
    b2 = np.ascontiguousarray(np.asarray(L0_ffn_b2, dtype=np.float32))
    cent = np.ascontiguousarray(np.asarray(L0_ffn_centroids, dtype=np.float32))
    wr = np.asarray(L0_ffn_w_route, dtype=np.float32)

    wt = np.ascontiguousarray(Wm.T)
    bqk = np.ascontiguousarray(b[0:2048].reshape(16, 128).T)
    bv = np.ascontiguousarray(b[2048:3072].reshape(1, 1024))
    b1c = np.ascontiguousarray(b1.reshape(8, 4, 128).transpose(2, 0, 1))
    wrt = np.ascontiguousarray(wr.T)

    nc, _ = get_nc()
    shared = {"wt": wt, "bqk": bqk, "bv": bv, "w1": w1, "b1": b1c,
              "w2": w2, "b2": b2, "cent": cent, "wrt": wrt}
    in_maps = [dict(shared, x=np.ascontiguousarray(x[i])) for i in range(8)]
    res = run_bass_kernel_spmd(nc, in_maps, core_ids=list(range(8)))
    out = np.stack([r["out"] for r in res.results], axis=0)
    return out


# version 26
# speedup vs baseline: 1.0019x; 1.0019x over previous
import contextlib
import numpy as np

import concourse.bass as bass
import concourse.mybir as mybir
import bass_rust
from concourse.bass_utils import run_bass_kernel_spmd

F32 = mybir.dt.float32
BF16 = mybir.dt.bfloat16
AF = mybir.ActivationFunctionType
OP = mybir.AluOpType
AX = mybir.AxisListType

T = 2048
C = 1024
NH = 16
NE = 8
DE = 512
NT = T // 128
NQ = T // 512
NC = C // 128
EPS = 1e-5

ASSEMBLY = "opt"
RSTAGE = 9
DEBUG_PHASE = None


class Plan:
    def __init__(self):
        self.ops = {e: [] for e in ("tensor", "vector", "scalar", "sync", "gpsimd")}

    def add(self, eng, fn, waits=()):
        self.ops[eng].append((list(waits), fn))


def emit_block(nc, plan, name=None):
    active = [e for e in plan.ops if plan.ops[e]]
    if not active:
        return
    with nc.Block(name) as block:
        for eng_name in active:
            ops = plan.ops[eng_name]

            def body(eng, ops=ops):
                for waits, fn in ops:
                    for s, v in waits:
                        eng.wait_ge(s, v() if callable(v) else v)
                    fn(eng)

            getattr(block, eng_name)(body)


def build_nc():
    nc = bass.Bass("TRN2", target_bir_lowering=False)

    x_d = nc.dram_tensor("x", [T, C], F32, kind="ExternalInput")
    wt_d = nc.dram_tensor("wt", [C, 3 * C], F32, kind="ExternalInput")
    bqk_d = nc.dram_tensor("bqk", [128, 16], F32, kind="ExternalInput")
    bv_d = nc.dram_tensor("bv", [1, C], F32, kind="ExternalInput")
    w1_d = nc.dram_tensor("w1", [NE, C, DE], F32, kind="ExternalInput")
    b1_d = nc.dram_tensor("b1", [128, NE, 4], F32, kind="ExternalInput")
    w2_d = nc.dram_tensor("w2", [NE, DE, C], F32, kind="ExternalInput")
    b2_d = nc.dram_tensor("b2", [NE, C], F32, kind="ExternalInput")
    cent_d = nc.dram_tensor("cent", [NE, C], F32, kind="ExternalInput")
    wrt_d = nc.dram_tensor("wrt", [C, NE], F32, kind="ExternalInput")
    out_d = nc.dram_tensor("out", [T, C], F32, kind="ExternalOutput")
    dbg_d = {}

    sem_names = [
        "s_t", "s_v", "s_s", "s_vg", "s_g",
        "d_cst", "d_gcst", "d_gdma",
        "d_x0", "d_x1", "d_x2", "d_x3",
        "d_xnT", "d_xn0", "d_xn1", "d_xn2", "d_xn3",
        "d_w10", "d_w11", "d_w20", "d_w21",
        "d_wt0", "d_wt1", "d_wt2", "d_wt3",
        "d_wv",
        "d_y0", "d_y1", "d_y2", "d_y3",
        "d_xf0", "d_xf1",
        "d_ob0", "d_ob1",
        "d_out",
    ]

    es = contextlib.ExitStack()
    with es:
        sem = {n: es.enter_context(nc.semaphore(n)) for n in sem_names}
        CNT = {n: 0 for n in sem_names}

        def mk(fn, semname=None, n=1):
            if semname is None:
                return fn, None
            prev = CNT[semname]
            CNT[semname] += n
            s = sem[semname]
            is_dma = semname.startswith("d_")

            def f(eng, fn=fn, s=s, n=n, prev=prev, is_dma=is_dma):
                if is_dma and prev > 0:
                    eng.wait_ge(s, prev)
                fn(eng).then_inc(s, n)
            return f, CNT[semname]

        def sb(name, shape, dt):
            return es.enter_context(nc.sbuf_tensor(name, shape, dt))

        def phase_fence(p):
            ww = [(sem[n], CNT[n]) for n in sem_names if n.startswith("d_") and CNT[n] > 0]
            for eng in ("tensor", "vector", "scalar", "sync", "gpsimd"):
                p.add(eng, lambda e: e.nop(), waits=list(ww))

        def W(*pairs):
            return [(sem[n], v) for n, v in pairs]

        iota_c = sb("iota_c", [128, 1], F32)
        iota_rb = sb("iota_rb", [128, 512], F32)
        mtri = sb("mtri", [128, 128], BF16)
        idm_bf = sb("idm_bf", [128, 128], BF16)
        idm_f = sb("idm_f", [16, 16], F32)
        ones_bf = sb("ones_bf", [1, 128], BF16)
        bqk_sb = sb("bqk_sb", [128, 16], F32)
        bqkq_sb = sb("bqkq_sb", [128, 8], F32)
        b1_sb = sb("b1_sb", [128, NE, 4], F32)
        bv_sb = sb("bv_sb", [1, C], BF16)
        b2_sb = sb("b2_sb", [NE, C], BF16)
        wcat = sb("wcat", [128, NC, 16], BF16)
        negm = sb("negm", [128, 16], F32)
        rstd_s = sb("rstd_s", [128, NT], F32)
        invn_s = sb("invn_s", [128, NT], F32)
        gates_tm = sb("gates_tm", [128, NT, NE], F32)
        gates_fm = sb("gates_fm", [NE, T], BF16)
        obk = sb("obk", [128, NC, 16], BF16)
        stp_c = sb("stp_c", [128, 1], F32)
        colv = sb("colv", [128, 1], F32)
        rec_sb = sb("rec_sb", [128, 64], F32)
        nmx = sb("nmx", [16, 16], F32)
        negcol = sb("negcol", [16, 1], F32)
        negrow = sb("negrow", [1, 16], F32)
        negrow_bf = sb("negrow_bf", [1, 16], BF16)

        acc = sb("acc", [128, NT, C], BF16)
        xn_fm = sb("xn_fm", [128, NC, T], BF16)

        wtr = sb("wtr", [128, 4, NC, 128], BF16)

        es_w = es.enter_context(contextlib.ExitStack())
        w1sb = es_w.enter_context(nc.sbuf_tensor("w1sb", [128, 2, NC, DE], BF16))
        w2sb = es_w.enter_context(nc.sbuf_tensor("w2sb", [128, 2, 4, C], BF16))
        es_b1 = es_w.enter_context(contextlib.ExitStack())
        cent_sb = es_b1.enter_context(nc.sbuf_tensor("cent_sb", [NE, C], F32))
        x_ring = es_b1.enter_context(nc.sbuf_tensor("x_ring", [128, 8, C], F32))
        xn_tm = es_b1.enter_context(nc.sbuf_tensor("xn_tm", [128, 8, C], BF16))

        def w1_load(ei):
            def f(e, ei=ei):
                return e.dma_start(
                    w1sb[:, ei % 2, :, :],
                    bass.AP(w1_d, ei * C * DE, [[DE, 128], [128 * DE, NC], [1, DE]]))
            return f

        def w2_load(ei):
            def f(e, ei=ei):
                return e.dma_start(
                    w2sb[:, ei % 2, :, :],
                    bass.AP(w2_d, ei * DE * C, [[C, 128], [128 * C, 4], [1, C]]))
            return f

        def wt_load(j):
            def f(e, j=j):
                return e.dma_start(
                    wtr[:, j % 4, :, :],
                    bass.AP(wt_d, j * 128, [[3 * C, 128], [128 * 3 * C, NC], [1, 128]]))
            return f

        def x_load(m):
            def f(e, m=m):
                return e.dma_start(
                    x_ring[:, m % 8, :],
                    bass.AP(x_d, m * 128 * C, [[C, 128], [1, C]]))
            return f

        p = Plan()
        p.add("gpsimd", mk(lambda e: e.iota(iota_c[:, :], [[1, 1]], channel_multiplier=1,
                                            allow_small_or_imprecise_dtypes=True), "d_gcst")[0])
        p.add("gpsimd", mk(lambda e: e.iota(iota_rb[:, :], [[1, 512]], channel_multiplier=0,
                                            allow_small_or_imprecise_dtypes=True), "d_gcst")[0])
        p.add("gpsimd", mk(lambda e: e.memset(acc[:, :, :], 0.0), "d_gcst")[0])
        p.add("gpsimd", mk(lambda e: e.memset(ones_bf[:, :], 1.0), "d_gcst")[0])
        gsetup = CNT["d_gcst"]

        p.add("vector", mk(lambda e: e.tensor_scalar(
            mtri[:, :], iota_rb[:, 0:128], iota_c[:, :], None, OP.is_ge), "s_v")[0],
            waits=W(("d_gcst", gsetup)))
        p.add("vector", mk(lambda e: e.tensor_scalar(
            idm_bf[:, :], iota_rb[:, 0:128], iota_c[:, :], None, OP.is_equal), "s_v")[0])
        p.add("vector", mk(lambda e: e.tensor_scalar(
            idm_f[:, :], iota_rb[0:16, 0:16], iota_c[0:16, :], None, OP.is_equal), "s_v")[0])
        fn, gob = mk(lambda e: e.tensor_scalar(
            stp_c[:, :], iota_c[:, :], 63.5, None, OP.is_gt), "s_vg")
        p.add("vector", fn)
        for cc in range(NC):
            fn, gob = mk(lambda e, cc=cc: e.tensor_scalar(
                colv[:, :], stp_c[:, :], float(2 * cc), None, OP.add), "s_vg")
            p.add("vector", fn, waits=W(("s_vg", gob - 1)))
            fn, gob = mk(lambda e, cc=cc: e.tensor_scalar(
                obk[:, cc, :], iota_rb[:, 0:16], colv[:, :], None, OP.is_equal), "s_vg")
            p.add("vector", fn, waits=W(("s_vg", gob - 1)))
        obk_done = gob

        p.add("sync", mk(lambda e: e.dma_start(cent_sb[:, :], cent_d[:, :]), "d_cst", 16)[0])
        cent_done = CNT["d_cst"]
        p.add("sync", mk(lambda e: e.dma_start(bqk_sb[:, :], bqk_d[:, :]), "d_cst", 16)[0])
        p.add("sync", mk(lambda e: e.dma_start(b1_sb[:, :, :], b1_d[:, :, :]), "d_cst", 16)[0])
        cst_b0 = CNT["d_cst"]
        p.add("gpsimd", mk(lambda e: e.dma_start(bv_sb[:, :], bv_d[:, :]), "d_gdma", 16)[0])
        p.add("gpsimd", mk(lambda e: e.dma_start(b2_sb[:, :], b2_d[:, :]), "d_gdma", 16)[0])
        p.add("gpsimd", mk(lambda e: e.dma_start(
            wcat[:, :, 8:16],
            bass.AP(wrt_d, 0, [[8, 128], [1024, 8], [1, 8]])), "d_gdma", 16)[0])

        p.add("vector", mk(lambda e: e.tensor_scalar(
            bqkq_sb[:, :], bqk_sb[:, 0:8], 0.125, None, OP.mult), "s_v")[0],
            waits=W(("d_cst", cst_b0)))

        w1_done, w2_done, wt_done, x_done = {}, {}, {}, {}
        for ei in (0, 1):
            fn, w1_done[ei] = mk(w1_load(ei), f"d_w1{ei % 2}", 16)
            p.add("gpsimd", fn)
            fn, w2_done[ei] = mk(w2_load(ei), f"d_w2{ei % 2}", 16)
            p.add("gpsimd", fn)
        for j in range(4):
            fn, wt_done[j] = mk(wt_load(j), f"d_wt{j % 4}", 16)
            p.add("gpsimd", fn)
        for m in range(8):
            fn, x_done[m] = mk(x_load(m), f"d_x{m % 4}", 16)
            p.add("sync", fn)
        emit_block(nc, p, "b0")

        bn6 = es_b1.enter_context(nc.sbuf_tensor("bn6", [128, 4, 12], F32))
        mvv = es_b1.enter_context(nc.sbuf_tensor("mvv", [128, NT, 2], F32))
        t1g = es_b1.enter_context(nc.sbuf_tensor("t1g", [128, 4], F32))
        t2b = es_b1.enter_context(nc.sbuf_tensor("t2b", [128, NT], F32))
        t3b = es_b1.enter_context(nc.sbuf_tensor("t3b", [128, NT], F32))
        nsq = es_b1.enter_context(nc.sbuf_tensor("nsq", [128, NT], F32))
        with contextlib.ExitStack() as es2:
            def sb2(name, shape, dt):
                return es2.enter_context(nc.sbuf_tensor(name, shape, dt))
            cn_bf = sb2("cn_bf", [NE, C], BF16)
            csq = sb2("csq", [NE, C], F32)
            cno = sb2("cno", [NE, 1], F32)
            gates_bf = sb2("gates_bf", [128, NT, NE], BF16)
            lg8 = sb2("lg8", [128, NE], F32)
            lgm = sb2("lgm", [128, NE], F32)
            lgs = sb2("lgs", [128, NT, NE], F32)
            lgraw = sb2("lgraw", [128, NT, 16], F32)
            lgk = sb2("lgk", [128, NT, NE], F32)
            m1c = sb2("m1c", [128, 1], F32)
            m2c = sb2("m2c", [128, 1], F32)
            ssum = sb2("ssum", [128, 1], F32)
            hsb = sb2("hsb", [128, 4, T], BF16)
            tps = es2.enter_context(nc.psum_tensor("tps", [128, 1024], BF16))
            hps = [es2.enter_context(nc.psum_tensor(f"hps{i}", [128, 512], F32)) for i in range(2)]
            ops_ = [es2.enter_context(nc.psum_tensor(f"ops{i}", [128, 512], F32)) for i in range(2)]
            b2ps = [es2.enter_context(nc.psum_tensor(f"b2ps{i}", [128, 512], F32)) for i in range(2)]
            lgps = b2ps

            SEG = {}
            class SegP:
                def __init__(self, name):
                    self.ops = {e: [] for e in ("tensor", "vector", "scalar", "sync", "gpsimd")}
                    SEG[name] = self
                def add(self, eng, fn, waits=()):
                    self.ops[eng].append((list(waits), fn))
            s_cn = SegP("cn")
            s_ln = [SegP(f"ln{m}") for m in range(NT)]
            s_lnB = [SegP(f"lnB{m}") for m in range(NT)]
            s_rA = [SegP(f"rA{m}") for m in range(NT)]
            s_rB = [SegP(f"rB{m}") for m in range(NT)]
            s_rC = [SegP(f"rC{m}") for m in range(NT)]
            s_ffn0 = SegP("ffn0")
            s_ffn1 = SegP("ffn1")

            p = s_cn
            cst_now = cent_done
            fn, g0 = mk(lambda e: e.tensor_tensor(csq[:, :], cent_sb[:, :], cent_sb[:, :], OP.mult), "s_vg")
            p.add("vector", fn, waits=W(("d_cst", cst_now)))
            fn, g1 = mk(lambda e: e.reduce_sum(cno[:, :], csq[:, :], AX.X), "s_vg")
            p.add("vector", fn, waits=W(("s_vg", g0)))
            fn, c_cno = mk(lambda e: e.nop(), "s_v")
            p.add("vector", fn, waits=W(("s_vg", g1)))
            fn, c_csqr = mk(lambda e: e.activation(cno[:, :], cno[:, :], AF.Sqrt), "s_s")
            p.add("scalar", fn, waits=W(("s_v", c_cno)))
            fn, g2 = mk(lambda e: e.tensor_scalar(cno[:, :], cno[:, :], 1e-12, None, OP.max), "s_vg")
            p.add("vector", fn, waits=W(("s_s", c_csqr)))
            fn, g3 = mk(lambda e: e.reciprocal(cno[:, :], cno[:, :]), "s_vg")
            p.add("vector", fn, waits=W(("s_vg", g2)))
            fn, c_cn = mk(lambda e: e.tensor_scalar(
                cn_bf[:, :], cent_sb[:, :], cno[:, :], None, OP.mult), "s_v")
            p.add("vector", fn, waits=W(("s_vg", g3)))
            ev_cn = {}
            for c in range(NC):
                w = W(("s_v", c_cn))
                if c > 0:
                    w += W(("s_s", ev_cn[c - 1]))
                fn, c_tp = mk(lambda e, c=c: e.transpose(
                    tps[:, 0:8], cn_bf[0:8, 128 * c:128 * (c + 1)], idm_bf[0:8, 0:8]), "s_t")
                p.add("tensor", fn, waits=w)
                fn, ev_cn[c] = mk(lambda e, c=c: e.activation(
                    wcat[:, c, 0:8], tps[:, 0:8], AF.Identity, bias=0.0, scale=1.0), "s_s")
                p.add("scalar", fn, waits=W(("s_t", c_tp)))

            xn_done = {}
            xn_t_done = {}
            c_rstdg, c_nsqg = {}, {}
            ga_fence = {}
            gb_fence = {}

            def var_ap(g):
                return bass.AP(mvv, 8 * g + 1, [[NT * 2, 128], [2, 4]])

            def plan_lnA(g):
                p = s_ln[g]
                gaggr = None
                for i in range(4):
                    m = 4 * g + i
                    w0 = W((f"d_x{m % 4}", x_done[m]))
                    if i == 0 and g >= 1:
                        w0 += W(("s_vg", ga_fence[g - 1]))
                    p.add("vector", mk(lambda e, i=i, m=m: e.bn_stats(
                        bn6[:, i, 0:6], x_ring[:, m % 8, 0:512]))[0], waits=w0)
                    fn, gs = mk(lambda e, i=i, m=m: e.bn_stats(
                        bn6[:, i, 6:12], x_ring[:, m % 8, 512:1024]), "s_vg")
                    p.add("vector", fn)
                for i in range(4):
                    m = 4 * g + i
                    fn, gaggr = mk(lambda e, i=i, m=m: e.bn_aggr(
                        mvv[:, m, :], bn6[:, i, :]), "s_vg")
                    p.add("vector", fn, waits=W(("s_vg", gs)) if i == 0 else ())
                fn, g1 = mk(lambda e, g=g: e.tensor_scalar(
                    t1g[:, :], var_ap(g), EPS, None, OP.add), "s_vg")
                wg1 = W(("s_vg", gaggr))
                if g >= 2:
                    wg1 += W(("s_vg", gb_fence[g - 2]))
                p.add("vector", fn, waits=wg1)
                fn, c_rvar4 = mk(lambda e, g=g: e.reciprocal(
                    t2b[:, 4 * g:4 * g + 4], t1g[:, :]), "s_v")
                p.add("vector", fn, waits=W(("s_vg", g1)))
                fn, g2 = mk(lambda e, g=g: e.scalar_tensor_tensor(
                    t3b[:, 4 * g:4 * g + 4], var_ap(g), float(C), t2b[:, 4 * g:4 * g + 4],
                    OP.mult, OP.mult), "s_vg")
                p.add("vector", fn, waits=W(("s_v", c_rvar4)))
                ga_fence[g] = g2
                fn, c_t3 = mk(lambda e: e.nop(), "s_v")
                p.add("vector", fn, waits=W(("s_vg", g2)))
                fn, c_rstdg[g] = mk(lambda e, g=g: e.activation(
                    rstd_s[:, 4 * g:4 * g + 4], t2b[:, 4 * g:4 * g + 4], AF.Sqrt), "s_s")
                p.add("scalar", fn, waits=W(("s_v", c_rvar4)))
                fn, c_nsqg[g] = mk(lambda e, g=g: e.activation(
                    nsq[:, 4 * g:4 * g + 4], t3b[:, 4 * g:4 * g + 4], AF.Sqrt), "s_s")
                p.add("scalar", fn, waits=W(("s_v", c_t3)))

            def plan_lnB(g):
                p = s_lnB[g]
                for i in range(4):
                    m = 4 * g + i
                    wxn = W(("s_s", c_rstdg[g])) if i == 0 else []
                    if m >= 8:
                        wxn = list(wxn) + [(sem[f"d_xn{m % 4}"], xn_t_done[m - 8])]
                    fn, c_xn = mk(lambda e, m=m: e.tensor_scalar(
                        xn_tm[:, m % 8, :], x_ring[:, m % 8, :], mvv[:, m, 0:1], rstd_s[:, m:m + 1],
                        OP.subtract, OP.mult), "s_v")
                    p.add("vector", fn, waits=wxn)
                    xn_done[m] = c_xn
                    if m + 8 < NT:
                        fn, x_done[m + 8] = mk(x_load(m + 8), f"d_x{(m + 8) % 4}", 16)
                        p.add("sync", fn, waits=W(("s_v", c_xn)))
                    fn, xn_t_done[m] = mk(lambda e, m=m: e.dma_start(
                        xn_fm[:, :, 128 * m:128 * (m + 1)], xn_tm[:, m % 8, :], transpose=True),
                        f"d_xn{m % 4}", 16)
                    p.add("sync", fn, waits=W(("s_v", c_xn)))
                w3 = W(("s_s", c_nsqg[g]))
                if g >= 1:
                    w3 += W(("s_vg", gb_fence[g - 1]))
                w3 += W(("s_vg", ga_fence[min(g + 1, 3)]))
                fn, g3 = mk(lambda e, g=g: e.tensor_scalar(
                    t1g[:, :], nsq[:, 4 * g:4 * g + 4], 1e-12, None, OP.max), "s_vg")
                p.add("vector", fn, waits=w3)
                fn, g4 = mk(lambda e, g=g: e.reciprocal(
                    invn_s[:, 4 * g:4 * g + 4], t1g[:, :]), "s_vg")
                p.add("vector", fn, waits=W(("s_vg", g3)))
                gb_fence[g] = g4

            ln_order = []
            for seq in (("A", 0), ("A", 1), ("B", 0), ("A", 2), ("B", 1), ("A", 3), ("B", 2), ("B", 3)):
                kind, g = seq
                if kind == "A":
                    plan_lnA(g)
                    ln_order.append(s_ln[g])
                else:
                    plan_lnB(g)
                    ln_order.append(s_lnB[g])

            def xn_wait_tile(m):
                return [(sem[f"d_xn{m % 4}"], xn_t_done[m])]

            def xn_wait_tt(tt):
                ww = []
                for i in range(4):
                    ww += xn_wait_tile(4 * tt + i)
                return ww

            XNALL = [(sem[f"d_xn{i}"], xn_t_done[12 + i]) for i in range(4)]

            gcst_all = CNT["d_gdma"]
            rmm_cnt = {}
            gp_cp = {}
            ra_fence = {}
            rb_fence = {}
            gfm_ev = {}
            c_gbf = {}
            c_sub = {}
            exp_rt = {}
            for m in range(NT if ASSEMBLY not in ("norouter", "lnonly") else 0):
                p = s_rA[m]
                bi = m % 2
                if RSTAGE < 1:
                    continue
                fn, g0 = mk(lambda e, m=m: e.tensor_scalar(
                    lg8[:, :], lgraw[:, m, 0:8], invn_s[:, m:m + 1], None, OP.mult), "s_vg")
                wg0 = [(sem["s_s"], lambda m=m: gp_cp[m])]
                if m >= 1:
                    wg0 += W(("s_vg", ra_fence[m - 1]))
                p.add("vector", fn, waits=wg0)
                fn, g1 = mk(lambda e, m=m: e.tensor_tensor(
                    lg8[:, :], lg8[:, :], lgraw[:, m, 8:16], OP.add), "s_vg")
                p.add("vector", fn, waits=W(("s_vg", g0)))
                if RSTAGE < 2:
                    continue
                fn, g2 = mk(lambda e: e.reduce_max(m1c[:, :], lg8[:, :], AX.X), "s_vg")
                p.add("vector", fn, waits=W(("s_vg", g1)))
                fn, g3 = mk(lambda e: e.tensor_scalar(
                    lgm[:, :], lg8[:, :], m1c[:, :], None, OP.is_ge), "s_vg")
                p.add("vector", fn, waits=W(("s_vg", g2)))
                fn, g4 = mk(lambda e: e.scalar_tensor_tensor(
                    lgm[:, :], lgm[:, :], -1e30, lg8[:, :], OP.mult, OP.add), "s_vg")
                p.add("vector", fn, waits=W(("s_vg", g3)))
                fn, g5 = mk(lambda e: e.reduce_max(m2c[:, :], lgm[:, :], AX.X), "s_vg")
                p.add("vector", fn, waits=W(("s_vg", g4)))
                fn, g6 = mk(lambda e, m=m: e.tensor_scalar(
                    lgs[:, m, :], lg8[:, :], m1c[:, :], None, OP.subtract), "s_vg")
                p.add("vector", fn, waits=W(("s_vg", g5)))
                fn, c_sub[m] = mk(lambda e: e.nop(), "s_v")
                p.add("vector", fn, waits=W(("s_vg", g6)))
                fn, g7 = mk(lambda e, m=m: e.tensor_scalar(
                    lgk[:, m, :], lg8[:, :], m2c[:, :], None, OP.is_ge), "s_vg")
                p.add("vector", fn)
                g7_rt = g7
                ra_fence[m] = g7

            ra_g7 = dict(ra_fence)
            for m in range(NT if (ASSEMBLY not in ("norouter", "lnonly") and RSTAGE >= 3) else 0):
                p = s_rB[m]
                fn, g8 = mk(lambda e, m=m: e.tensor_tensor(
                    lgk[:, m, :], lgk[:, m, :], lgs[:, m, :], OP.mult), "s_vg")
                wg8 = [(sem["s_s"], lambda m=m: exp_rt[m])] + W(("s_vg", ra_g7[m]))
                if m >= 1 and (m - 1) in rb_fence:
                    wg8 += W(("s_vg", rb_fence[m - 1]))
                p.add("vector", fn, waits=wg8)
                if RSTAGE < 3.2:
                    continue
                fn, g9 = mk(lambda e, m=m: e.reduce_sum(ssum[:, :], lgk[:, m, :], AX.X), "s_vg")
                p.add("vector", fn, waits=W(("s_vg", g8)))
                if RSTAGE < 3.4:
                    continue
                fn, g10 = mk(lambda e: e.reciprocal(ssum[:, :], ssum[:, :]), "s_vg")
                p.add("vector", fn, waits=W(("s_vg", g9)))
                if RSTAGE < 3.8:
                    continue
                fn, g11 = mk(lambda e, m=m: e.tensor_scalar(
                    gates_tm[:, m, :], lgk[:, m, :], ssum[:, :], None, OP.mult), "s_vg")
                p.add("vector", fn, waits=W(("s_vg", g10)))
                rb_fence[m] = g11
                fn, c_gbf[m] = mk(lambda e, m=m: e.tensor_copy(
                    gates_bf[:, m, :], gates_tm[:, m, :]), "s_v")
                p.add("vector", fn, waits=W(("s_vg", g11)))
            c_gel, c_stt = {}, {}
            c_hmm_all, c_omm_all = {}, {}
            ffn_experts = list(range(NE if ASSEMBLY not in ("noffn", "seqnoffn", "lnonly") else 0))
            def plan_expert(ei, p, mid=None):
                sl = ei % 2
                for g in range(16):
                    hc, tt = divmod(g, 4)
                    bi = g % 2
                    w = []
                    if g == 0:
                        w += W((f"d_w1{sl}", w1_done[ei]))
                    if ei == 0:
                        w += xn_wait_tt(tt)
                    if (ei, g - 2) in c_gel:
                        w += W(("s_s", c_gel[(ei, g - 2)]))
                    elif ei > 0 and g < 2:
                        w += W(("s_s", c_gel[(ei - 1, 14 + g)]))
                    for c in range(NC):
                        fn, c_mm = mk(lambda e, sl=sl, hc=hc, tt=tt, bi=bi, c=c: e.matmul(
                            hps[bi][:, :], w1sb[:, sl, c, 128 * hc:128 * (hc + 1)],
                            xn_fm[:, c, 512 * tt:512 * (tt + 1)],
                            start=(c == 0), stop=(c == NC - 1)), "s_t")
                        p.add("tensor", fn, waits=w if c == 0 else ())
                    if g == 15:
                        c_hmm_all[ei] = c_mm
                    w2w = W(("s_t", c_mm))
                    if ei >= 1 and g == 0:
                        w2w += W(("s_t", c_omm_all[ei - 1]))
                    fn, c_gel[(ei, g)] = mk(lambda e, ei=ei, hc=hc, tt=tt, bi=bi: e.activation(
                        hsb[:, hc, 512 * tt:512 * (tt + 1)], hps[bi][:, :], AF.Gelu,
                        bias=b1_sb[:, ei, hc:hc + 1], scale=1.0), "s_s")
                    p.add("scalar", fn, waits=w2w)
                if ei + 2 < NE:
                    fn, w1_done[ei + 2] = mk(w1_load(ei + 2), f"d_w1{ei % 2}", 16)
                    p.add("gpsimd", fn, waits=W(("s_t", c_hmm_all[ei])))
                if mid is not None:
                    mid(p)
                for og in range(32):
                    t2, ch = divmod(og, 2)
                    bi = og % 2
                    w = []
                    if og == 0:
                        w += W((f"d_w2{sl}", w2_done[ei]), ("s_s", c_gel[(ei, 15)]))
                    if (ei, og - 2) in c_stt:
                        w += W(("s_v", c_stt[(ei, og - 2)]))
                    elif ei > 0 and og < 2:
                        w += W(("s_v", c_stt[(ei - 1, 30 + og)]))
                    for hc in range(4):
                        fn, c_mm = mk(lambda e, t2=t2, ch=ch, bi=bi, hc=hc, sl=sl: e.matmul(
                            ops_[bi][:, :], hsb[:, hc, 128 * t2:128 * (t2 + 1)],
                            w2sb[:, sl, hc, 512 * ch:512 * (ch + 1)],
                            start=(hc == 0), stop=(hc == 3)), "s_t")
                        p.add("tensor", fn, waits=w if hc == 0 else ())
                    if og == 31:
                        c_omm_all[ei] = c_mm
                    fn, c_stt[(ei, og)] = mk(lambda e, ei=ei, t2=t2, ch=ch, bi=bi: e.scalar_tensor_tensor(
                        acc[:, t2, 512 * ch:512 * (ch + 1)], ops_[bi][:, :],
                        gates_tm[:, t2, ei:ei + 1], acc[:, t2, 512 * ch:512 * (ch + 1)],
                        OP.mult, OP.add), "s_v")
                    stt_w = W(("s_t", c_mm))
                    if ei == 0 and og == 0:
                        stt_w += W(("d_gcst", gsetup))
                    p.add("vector", fn, waits=stt_w)
                if ei + 2 < NE:
                    fn, w2_done[ei + 2] = mk(w2_load(ei + 2), f"d_w2{ei % 2}", 16)
                    p.add("gpsimd", fn, waits=W(("s_t", c_omm_all[ei])))
            def plan_router(p):
                for m in range(NT):
                    w = xn_wait_tile(m) + W(("s_s", ev_cn[NC - 1]), ("d_gdma", gcst_all))
                    if m >= 2:
                        w += [(sem["s_s"], lambda m=m: gp_cp[m - 2])]
                    bi = m % 2
                    c_mm = None
                    for c in range(NC):
                        fn, c_mm = mk(lambda e, m=m, c=c, bi=bi: e.matmul(
                            lgps[bi][:, 0:16], xn_fm[:, c, 128 * m:128 * (m + 1)], wcat[:, c, :],
                            start=(c == 0), stop=(c == NC - 1)), "s_t")
                        p.add("tensor", fn, waits=w if c == 0 else ())
                    rmm_cnt[m] = c_mm
                    fn, gp_cp[m] = mk(lambda e, m=m, bi=bi: e.copy(
                        lgraw[:, m, :], lgps[bi][:, 0:16]), "s_s")
                    p.add("scalar", fn, waits=W(("s_t", c_mm)))
                for m in range(NT):
                    fn, exp_rt[m] = mk(lambda e, m=m: e.activation(
                        lgs[:, m, :], lgs[:, m, :], AF.Exp), "s_s")
                    p.add("scalar", fn, waits=W(("s_v", c_sub[m])))
            for ei in ffn_experts[:1]:
                plan_expert(ei, s_ffn0, mid=plan_router)
            c_gtp = {}
            if RSTAGE < 4:
                pass
            for m in range(NT if (ASSEMBLY not in ("norouter", "lnonly") and RSTAGE >= 4) else 0):
                p = s_rC[m]
                w = W(("s_v", c_gbf[m]))
                if m >= 1:
                    w += W(("s_v", gfm_ev[m - 1]))
                if m == 0:
                    w += W(("s_s", ev_cn[NC - 1]))
                fn, c_gtp[m] = mk(lambda e, m=m: e.transpose(
                    tps[0:8, 128:256], gates_bf[:, m, :], idm_bf[:, :]), "s_t")
                p.add("tensor", fn, waits=w)
                fn, gfm_ev[m] = mk(lambda e, m=m: e.tensor_copy(
                    gates_fm[:, 128 * m:128 * (m + 1)], tps[0:8, 128:256]), "s_v")
                p.add("vector", fn, waits=W(("s_t", c_gtp[m])))
            for ei in ffn_experts[1:]:
                plan_expert(ei, s_ffn1)
            p = s_ffn1
            c_b2ev = {}
            for og in range(32 if ASSEMBLY not in ("noffn", "seqnoffn", "norouter", "lnonly") else 0):
                t2, ch = divmod(og, 2)
                bi2 = og % 2
                w = []
                if og == 0:
                    w += [(sem["s_s"], lambda: gp_cp[NT - 1])]
                    w += W(("d_gdma", gcst_all), ("s_v", gfm_ev[NT - 1]))
                if og >= 2:
                    w += W(("s_v", c_b2ev[og - 2]))
                fn, c_mm = mk(lambda e, t2=t2, ch=ch, bi2=bi2: e.matmul(
                    b2ps[bi2][:, :], gates_fm[0:8, 128 * t2:128 * (t2 + 1)],
                    b2_sb[0:8, 512 * ch:512 * (ch + 1)], start=True, stop=True), "s_t")
                p.add("tensor", fn, waits=w)
                fn, c_b2ev[og] = mk(lambda e, t2=t2, ch=ch, bi2=bi2: e.tensor_tensor(
                    acc[:, t2, 512 * ch:512 * (ch + 1)], acc[:, t2, 512 * ch:512 * (ch + 1)],
                    b2ps[bi2][:, :], OP.add), "s_v")
                p.add("vector", fn, waits=W(("s_t", c_mm)))

            p = Plan()
            def cat(eng, *segs):
                for s in segs:
                    p.ops[eng].extend(s.ops[eng])
            cat("sync", s_cn, *ln_order)
            cat("gpsimd", s_ffn0, s_ffn1)
            cat("tensor", s_cn, *s_rA, s_ffn0, *s_rC, s_ffn1)
            cat("vector", s_cn, *ln_order, *s_rA, *s_rB, s_ffn0, *s_rC, s_ffn1)
            cat("scalar", s_cn, *ln_order, *s_rA, s_ffn0, s_ffn1)
            emit_block(nc, p, "b123")
        es_b1.close()
        es_w.close()

        def finish_dbg(p):
            p.ops["sync"].append((W(("d_out", CNT["d_out"])), lambda e: e.nop()))
            emit_block(nc, p, "dbg")
            return nc, dbg_d

        if DEBUG_PHASE == "b123":
            dbg_d["dbg_acc"] = nc.dram_tensor("dbg_acc", [128, NT, C], BF16, kind="ExternalOutput")
            p = Plan()
            p.add("sync", mk(lambda e: e.dma_start(
                dbg_d["dbg_acc"][:, :, :], acc[:, :, :]), "d_out", 16)[0],
                waits=W(("s_v", CNT["s_v"]), ("s_s", CNT["s_s"]), ("s_t", CNT["s_t"])))
            return finish_dbg(p)

        es_qkv = es.enter_context(contextlib.ExitStack())
        q_fm = es_qkv.enter_context(nc.sbuf_tensor("q_fm", [128, NC, T], BF16))
        k_fm = es_qkv.enter_context(nc.sbuf_tensor("k_fm", [128, NC, T], BF16))
        v_sb = es_qkv.enter_context(nc.sbuf_tensor("v_sb", [128, NT, NH * 65], BF16))

        with contextlib.ExitStack() as es4:
            qps = [es4.enter_context(nc.psum_tensor(f"qps{i}", [128, 512], F32)) for i in range(4)]
            nps = es4.enter_context(nc.psum_tensor("nps", [128, 16], F32))
            sqb = es4.enter_context(nc.sbuf_tensor("sqb", [128, 2, T], BF16))

            p = Plan()
            phase_fence(p)
            p.add("gpsimd", mk(lambda e: e.memset(
                bass.AP(v_sb, 64, [[NT * NH * 65, 128], [NH * 65, NT], [65, NH]]), 1.0),
                "d_gcst")[0])
            vones = CNT["d_gcst"]

            c_qkev = {}
            gidx = 0
            for j in range(16):
                slot = j % 4
                for tt in range(4):
                    bi = gidx % 4
                    w = []
                    if tt == 0:
                        w += W((f"d_wt{slot}", wt_done[j]))
                        if j == 0:
                            w += XNALL
                    if gidx - 4 >= 0:
                        w += W(("s_s", c_qkev[gidx - 4]))
                    for c in range(NC):
                        fn, c_mm = mk(lambda e, slot=slot, c=c, tt=tt, bi=bi: e.matmul(
                            qps[bi][:, :], wtr[:, slot, c, :], xn_fm[:, c, 512 * tt:512 * (tt + 1)],
                            start=(c == 0), stop=(c == NC - 1)), "s_t")
                        p.add("tensor", fn, waits=w if c == 0 else ())
                    if j < 8:
                        fn, c_qkev[gidx] = mk(lambda e, j=j, tt=tt, bi=bi: e.activation(
                            q_fm[:, j, 512 * tt:512 * (tt + 1)], qps[bi][:, :], AF.Identity,
                            bias=bqkq_sb[:, j:j + 1], scale=0.125), "s_s")
                    else:
                        fn, c_qkev[gidx] = mk(lambda e, j=j, tt=tt, bi=bi: e.activation(
                            k_fm[:, j - 8, 512 * tt:512 * (tt + 1)], qps[bi][:, :], AF.Identity,
                            bias=bqk_sb[:, j:j + 1], scale=1.0), "s_s")
                    p.add("scalar", fn, waits=W(("s_t", c_mm)))
                    gidx += 1
                if j + 4 < 20:
                    fn, wt_done[j + 4] = mk(wt_load(j + 4), f"d_wt{(j + 4) % 4}", 16)
                    p.add("gpsimd", fn, waits=W(("s_t", c_mm)))
            c_vev = {}
            for vh in range(2):
                for t2 in range(NT):
                    vgi = 16 * vh + t2
                    bi = vgi % 4
                    w = []
                    if t2 == 0:
                        for jj in range(16 + 4 * vh, 20 + 4 * vh):
                            w += W((f"d_wt{jj % 4}", wt_done[jj]))
                        if vh == 0:
                            w += W(("d_gcst", vones))
                    if vgi - 4 >= 0:
                        w += W(("s_s", c_vev[vgi - 4]))
                    else:
                        w += W(("s_s", c_qkev[60 + vgi]))
                    for c in range(NC):
                        fn, c_mm = mk(lambda e, t2=t2, bi=bi, c=c: e.matmul(
                            qps[bi][:, :], xn_fm[:, c, 128 * t2:128 * (t2 + 1)],
                            bass.AP(wtr, c * 128, [[4 * NC * 128, 128], [NC * 128, 4], [1, 128]]),
                            start=(c == 0), stop=False), "s_t")
                        p.add("tensor", fn, waits=w if c == 0 else ())
                    fn, c_mm = mk(lambda e, vh=vh, bi=bi: e.matmul(
                        qps[bi][:, :], ones_bf[:, :], bv_sb[:, 512 * vh:512 * (vh + 1)],
                        start=False, stop=True), "s_t")
                    p.add("tensor", fn)
                    fn, c_vev[vgi] = mk(lambda e, t2=t2, vh=vh, bi=bi: e.activation(
                        bass.AP(v_sb, t2 * NH * 65 + vh * 8 * 65,
                                [[NT * NH * 65, 128], [65, 8], [1, 64]]),
                        qps[bi][:, :], AF.Identity, bias=0.0, scale=1.0), "s_s")
                    p.add("scalar", fn, waits=W(("s_t", c_mm)))
                if vh == 0:
                    for jj in range(20, 24):
                        fn, wt_done[jj] = mk(wt_load(jj), f"d_wt{jj % 4}", 16)
                        p.add("gpsimd", fn, waits=W(("s_t", c_mm)))
            norm_section = DEBUG_PHASE != "qkv0"
            if norm_section:
                c_sq = {}
                c_nmm = {}
                c_nred = {}
                last_vev = c_vev[31]
                for si, srcb in enumerate((q_fm, k_fm)):
                    for cc in range(NC):
                        sl2 = (si * NC + cc) % 2
                        w = []
                        if si == 0 and cc == 0:
                            w += W(("s_s", c_qkev[63]), ("s_vg", obk_done))
                        if (si * NC + cc) >= 2:
                            w += W(("s_t", c_nmm[si * NC + cc - 2]))
                        fn, c_sq[si * NC + cc] = mk(lambda e, srcb=srcb, cc=cc, sl2=sl2: e.tensor_tensor(
                            sqb[:, sl2, :], srcb[:, cc, :], srcb[:, cc, :], OP.mult), "s_v")
                        p.add("vector", fn, waits=w)
                        for tt in range(4):
                            wmm = []
                            if cc == 0 and tt == 0:
                                wmm += W(("s_v", c_sq[si * NC + cc]))
                                if si == 0:
                                    wmm += W(("s_s", last_vev))
                                else:
                                    wmm += W(("s_v", c_nred[3]))
                            elif tt == 0:
                                wmm += W(("s_v", c_sq[si * NC + cc]))
                            fn, c_mm2 = mk(lambda e, cc=cc, tt=tt, sl2=sl2: e.matmul(
                                qps[tt][0:16, :], obk[:, cc, :], sqb[:, sl2, 512 * tt:512 * (tt + 1)],
                                start=(cc == 0), stop=(cc == NC - 1)), "s_t")
                            p.add("tensor", fn, waits=wmm)
                        c_nmm[si * NC + cc] = c_mm2
                    gv0 = None
                    for tt in range(4):
                        fn, gv0 = mk(lambda e, si=si, tt=tt: e.reduce_max(
                            nmx[:, 8 * si + tt:8 * si + tt + 1], qps[tt][0:16, :], AX.X), "s_vg")
                        p.add("vector", fn, waits=W(("s_t", c_nmm[si * NC + NC - 1])))
                        c_nred[tt] = gv0 if si == 0 else c_nred[tt]
                    fn, gv0 = mk(lambda e, si=si: e.tensor_tensor(
                        nmx[:, 8 * si:8 * si + 1], nmx[:, 8 * si:8 * si + 1],
                        nmx[:, 8 * si + 1:8 * si + 2], OP.max), "s_vg")
                    p.add("vector", fn, waits=W(("s_vg", gv0 - 1)))
                    fn, gv0 = mk(lambda e, si=si: e.tensor_tensor(
                        nmx[:, 8 * si + 2:8 * si + 3], nmx[:, 8 * si + 2:8 * si + 3],
                        nmx[:, 8 * si + 3:8 * si + 4], OP.max), "s_vg")
                    p.add("vector", fn, waits=W(("s_vg", gv0 - 1)))
                    fn, gv0 = mk(lambda e, si=si: e.tensor_tensor(
                        nmx[:, 8 * si:8 * si + 1], nmx[:, 8 * si:8 * si + 1],
                        nmx[:, 8 * si + 2:8 * si + 3], OP.max), "s_vg")
                    p.add("vector", fn, waits=W(("s_vg", gv0 - 1)))
                    fn, c_vmax = mk(lambda e: e.nop(), "s_v")
                    p.add("vector", fn, waits=W(("s_vg", gv0)))
                    if si == 0:
                        c_nred[3] = c_vmax
                fn, c_sq1 = mk(lambda e: e.activation(
                    nmx[0:16, 4:5], nmx[0:16, 0:1], AF.Sqrt), "s_s")
                p.add("scalar", fn, waits=W(("s_v", c_vmax)))
                fn, c_sq2 = mk(lambda e: e.activation(
                    nmx[0:16, 5:6], nmx[0:16, 8:9], AF.Sqrt), "s_s")
                p.add("scalar", fn)
                fn, gn = mk(lambda e: e.tensor_tensor(
                    negcol[:, :], nmx[0:16, 4:5], nmx[0:16, 5:6], OP.mult), "s_vg")
                p.add("vector", fn, waits=W(("s_s", c_sq2)))
                fn, c_nc = mk(lambda e: e.tensor_scalar(
                    negcol[:, :], negcol[:, :], -1.0, None, OP.mult), "s_v")
                p.add("vector", fn, waits=W(("s_vg", gn)))
                fn, c_ntp = mk(lambda e: e.transpose(
                    nps[0:1, 0:16], negcol[:, :], idm_f[:, :]), "s_t")
                p.add("tensor", fn, waits=W(("s_v", c_nc)))
                fn, gn = mk(lambda e: e.tensor_copy(negrow[:, :], nps[0:1, 0:16]), "s_vg")
                p.add("vector", fn, waits=W(("s_t", c_ntp)))
                fn, c_nr = mk(lambda e: e.tensor_copy(negrow_bf[:, :], negrow[:, :]), "s_v")
                p.add("vector", fn, waits=W(("s_vg", gn)))
                fn, c_nmmb = mk(lambda e: e.matmul(
                    nps[:, :], ones_bf[:, :], negrow_bf[:, :], start=True, stop=True), "s_t")
                p.add("tensor", fn, waits=W(("s_v", c_nr)))
                fn, c_nev = mk(lambda e: e.tensor_copy(negm[:, :], nps[:, :]), "s_v")
                p.add("vector", fn, waits=W(("s_t", c_nmmb)))
            emit_block(nc, p, "b4_qkv")

        if DEBUG_PHASE in ("qkv", "qkv0"):
            dbg_d["dbg_q"] = nc.dram_tensor("dbg_q", [128, NC, T], BF16, kind="ExternalOutput")
            dbg_d["dbg_k"] = nc.dram_tensor("dbg_k", [128, NC, T], BF16, kind="ExternalOutput")
            dbg_d["dbg_v"] = nc.dram_tensor("dbg_v", [128, NT, NH * 65], BF16, kind="ExternalOutput")
            dbg_d["dbg_negm"] = nc.dram_tensor("dbg_negm", [128, 16], F32, kind="ExternalOutput")
            p = Plan()
            p.add("sync", mk(lambda e: e.dma_start(dbg_d["dbg_q"][:, :, :], q_fm[:, :, :]), "d_out", 16)[0])
            p.add("sync", mk(lambda e: e.dma_start(dbg_d["dbg_k"][:, :, :], k_fm[:, :, :]), "d_out", 16)[0])
            p.add("sync", mk(lambda e: e.dma_start(dbg_d["dbg_v"][:, :, :], v_sb[:, :, :]), "d_out", 16)[0])
            p.add("sync", mk(lambda e: e.dma_start(dbg_d["dbg_negm"][:, :], negm[:, :]), "d_out", 16)[0])
            return finish_dbg(p)

        with contextlib.ExitStack() as es5:
            pt = es5.enter_context(nc.sbuf_tensor("pt", [128, 4, 512], BF16))
            ystage = es5.enter_context(nc.sbuf_tensor("ystage", [80, 4, 512], BF16))
            yqt = es5.enter_context(nc.sbuf_tensor("yqt", [128, 4, NH * 80], BF16))
            xf = es5.enter_context(nc.sbuf_tensor("xf", [128, 2, C], F32))
            sps = [es5.enter_context(nc.psum_tensor(f"sps{i}", [128, 512], F32)) for i in range(4)]
            pvp = [es5.enter_context(nc.psum_tensor(f"pvp{i}", [65, 512], F32)) for i in range(4)]

            p = Plan()
            phase_fence(p)
            fn, c_ystz = mk(lambda e: e.memset(ystage[64:80, :, :], 0.0), "s_v")
            p.add("vector", fn)
            tj = 0
            exp_cnt, msk_cnt, pv_cnt = {}, {}, {}
            yev_cnt = {}
            yT_uses = {0: 0, 1: 0, 2: 0, 3: 0}
            acc_cnt = {}
            add_cnt = {}

            for qt in range(NQ):
                nkb = 4 * qt + 4
                for hp in range(8):
                    ghA = 16 * qt + 2 * hp
                    hA, hB = 2 * hp, 2 * hp + 1
                    gpar = (8 * qt + hp) % 2
                    pvA, pvB = pvp[2 * gpar], pvp[2 * gpar + 1]
                    sc_of = {}
                    for kb in range(nkb):
                        j = kb - 4 * qt
                        n0 = 128 * j if j >= 0 else 0
                        biA = sps[2 * (tj % 2)]
                        biB = sps[2 * (tj % 2) + 1]
                        w = []
                        if tj >= 2:
                            w += W(("s_s", exp_cnt[2 * (tj - 2) + 1]))
                        fn, _ = mk(lambda e, hp=hp, kb=kb, qt=qt, n0=n0, biA=biA: e.matmul(
                            biA[:, n0:512], k_fm[0:64, hp, 128 * kb:128 * (kb + 1)],
                            q_fm[0:64, hp, 512 * qt + n0:512 * (qt + 1)],
                            start=True, stop=True), "s_t")
                        p.add("tensor", fn, waits=w)
                        fn, cB = mk(lambda e, hp=hp, kb=kb, qt=qt, n0=n0, biB=biB: e.matmul(
                            biB[:, n0:512], k_fm[64:128, hp, 128 * kb:128 * (kb + 1)],
                            q_fm[64:128, hp, 512 * qt + n0:512 * (qt + 1)],
                            start=True, stop=True, tile_position=(64, 0)), "s_t")
                        p.add("tensor", fn)
                        sc_of[kb] = (tj, n0)
                        for which, (bank, head) in enumerate(((biA, hA), (biB, hB))):
                            ht = 2 * tj + which
                            slot = ht % 4
                            wS = W(("s_t", cB))
                            if ht - 4 >= 0:
                                wS += W(("s_t", pv_cnt[ht - 4]))
                            fn, exp_cnt[ht] = mk(lambda e, bank=bank, head=head, slot=slot, n0=n0: e.activation(
                                pt[:, slot, n0:512], bank[:, n0:512], AF.Exp,
                                bias=negm[:, head:head + 1], scale=1.0), "s_s")
                            p.add("scalar", fn, waits=wS)
                        if j >= 0:
                            for which in range(2):
                                ht = 2 * tj + which
                                slot = ht % 4
                                fn, msk_cnt[ht] = mk(lambda e, slot=slot, n0=n0: e.tensor_tensor(
                                    pt[:, slot, n0:n0 + 128], pt[:, slot, n0:n0 + 128],
                                    mtri[:, :], OP.mult), "s_v")
                                mw = W(("s_s", exp_cnt[ht]))
                                if tj == 0:
                                    mw += W(("d_gcst", gsetup))
                                p.add("vector", fn, waits=mw)
                        pv_list = []
                        if kb >= 1:
                            pv_list.append(kb - 1)
                        if kb == nkb - 1:
                            pv_list.append(kb)
                        for kpv in pv_list:
                            tjp, n0p = sc_of[kpv]
                            jp = kpv - 4 * qt
                            for which, (head, pvbank) in enumerate(((hA, pvA), (hB, pvB))):
                                htp = 2 * tjp + which
                                slot = htp % 4
                                wP = []
                                if jp >= 0:
                                    wP += W(("s_v", msk_cnt[htp]))
                                else:
                                    wP += W(("s_s", exp_cnt[htp]))
                                if kpv == 0 and ghA - 4 + which >= 0:
                                    wP += W(("s_v", yev_cnt.get(ghA - 4 + which, 0)))
                                fn, pv_cnt[htp] = mk(lambda e, head=head, pvbank=pvbank, kpv=kpv,
                                                     slot=slot, n0p=n0p, nkb=nkb: e.matmul(
                                    pvbank[:, n0p:512],
                                    v_sb[:, kpv, 65 * head:65 * head + 65],
                                    pt[:, slot, n0p:512],
                                    start=(kpv == 0), stop=(kpv == nkb - 1)), "s_t")
                                p.add("tensor", fn, waits=wP)
                        tj += 1
                    lastA = 2 * (tj - 1)
                    for which, (pvbank,) in enumerate(((pvA,), (pvB,))):
                        gh = ghA + which
                        head = 2 * hp + which
                        ysl = gh % 4
                        wv_ = W(("s_t", pv_cnt[lastA + which]))
                        if yT_uses[ysl] > 0:
                            wv_ += W((f"d_y{ysl}", 16 * yT_uses[ysl]))
                        fn, yev_cnt[gh] = mk(lambda e, ysl=ysl, pvbank=pvbank: e.tensor_copy(
                            ystage[0:65, ysl, :], pvbank[0:65, :]), "s_v")
                        p.add("vector", fn, waits=wv_)
                        wt2 = W(("s_v", yev_cnt[gh]))
                        if qt > 0:
                            wt2 += W(("s_v", acc_cnt[qt - 1]))
                        fn, _ = mk(lambda e, ysl=ysl, head=head: e.dma_start(
                            yqt[:, :, 80 * head:80 * (head + 1)], ystage[:, ysl, :],
                            transpose=True), f"d_y{ysl}", 16)
                        p.add("sync", fn, waits=wt2)
                        yT_uses[ysl] += 1
                wA = []
                for sl4 in range(4):
                    wA += W((f"d_y{sl4}", 16 * yT_uses[sl4]))
                g1 = None
                for u in range(4):
                    fn, g1 = mk(lambda e, u=u: e.reciprocal(
                        rec_sb[:, 16 * u:16 * (u + 1)],
                        bass.AP(yqt, u * NH * 80 + 64, [[4 * NH * 80, 128], [80, 16]])), "s_vg")
                    p.add("vector", fn, waits=wA if u == 0 else ())
                wA2 = W(("s_vg", g1))
                first = True
                for u in range(4):
                    for h in range(NH):
                        fn, c_add = mk(lambda e, qt=qt, u=u, h=h: e.scalar_tensor_tensor(
                            acc[:, 4 * qt + u, 64 * h:64 * (h + 1)],
                            yqt[:, u, 80 * h:80 * h + 64],
                            rec_sb[:, 16 * u + h:16 * u + h + 1],
                            acc[:, 4 * qt + u, 64 * h:64 * (h + 1)],
                            OP.mult, OP.add), "s_v")
                        p.add("vector", fn, waits=wA2 if first else ())
                        first = False
                acc_cnt[qt] = c_add
                for u in range(4):
                    m = 4 * qt + u
                    sl = m % 2
                    w = []
                    if m >= 2:
                        w += W((f"d_ob{sl}", 16 * (m // 2)))
                    fn, xf_done = mk(lambda e, m=m, sl=sl: e.dma_start(
                        xf[:, sl, :], bass.AP(x_d, m * 128 * C, [[C, 128], [1, C]])),
                        f"d_xf{sl}", 16)
                    p.add("gpsimd", fn, waits=w)
                    wv_ = W((f"d_xf{sl}", xf_done), ("s_v", acc_cnt[qt]))
                    fn, add_cnt[m] = mk(lambda e, m=m, sl=sl: e.tensor_add(
                        xf[:, sl, :], xf[:, sl, :], acc[:, m, :]), "s_g")
                    p.add("gpsimd", fn, waits=wv_)
                    fn, _ = mk(lambda e, m=m, sl=sl: e.dma_start(
                        bass.AP(out_d, m * 128 * C, [[C, 128], [1, C]]), xf[:, sl, :]),
                        f"d_ob{sl}", 16)
                    p.add("gpsimd", fn, waits=W(("s_g", add_cnt[m])))
            p.ops["gpsimd"].append(
                (W(("d_ob0", CNT["d_ob0"]), ("d_ob1", CNT["d_ob1"])), lambda e: e.nop()))
            emit_block(nc, p, "b5_attn")
        es_qkv.close()


    return nc, dbg_d


_NC_CACHE = {}


def get_nc():
    key = DEBUG_PHASE
    if key not in _NC_CACHE:
        _NC_CACHE[key] = build_nc()
    return _NC_CACHE[key]


def kernel(L0_attn_x, L0_attn_W, L0_attn_b, L0_ffn_w1, L0_ffn_b1, L0_ffn_w2,
           L0_ffn_b2, L0_ffn_centroids, L0_ffn_w_route):
    x = np.ascontiguousarray(np.asarray(L0_attn_x, dtype=np.float32))
    Wm = np.asarray(L0_attn_W, dtype=np.float32)
    b = np.asarray(L0_attn_b, dtype=np.float32)
    w1 = np.ascontiguousarray(np.asarray(L0_ffn_w1, dtype=np.float32))
    b1 = np.asarray(L0_ffn_b1, dtype=np.float32)
    w2 = np.ascontiguousarray(np.asarray(L0_ffn_w2, dtype=np.float32))
    b2 = np.ascontiguousarray(np.asarray(L0_ffn_b2, dtype=np.float32))
    cent = np.ascontiguousarray(np.asarray(L0_ffn_centroids, dtype=np.float32))
    wr = np.asarray(L0_ffn_w_route, dtype=np.float32)

    wt = np.ascontiguousarray(Wm.T)
    bqk = np.ascontiguousarray(b[0:2048].reshape(16, 128).T)
    bv = np.ascontiguousarray(b[2048:3072].reshape(1, 1024))
    b1c = np.ascontiguousarray(b1.reshape(8, 4, 128).transpose(2, 0, 1))
    wrt = np.ascontiguousarray(wr.T)

    nc, _ = get_nc()
    shared = {"wt": wt, "bqk": bqk, "bv": bv, "w1": w1, "b1": b1c,
              "w2": w2, "b2": b2, "cent": cent, "wrt": wrt}
    in_maps = [dict(shared, x=np.ascontiguousarray(x[i])) for i in range(8)]
    res = run_bass_kernel_spmd(nc, in_maps, core_ids=list(range(8)))
    out = np.stack([r["out"] for r in res.results], axis=0)
    return out


# version 27 (final)
# speedup vs baseline: 1.0235x; 1.0216x over previous
import contextlib
import numpy as np

import concourse.bass as bass
import concourse.mybir as mybir
import bass_rust
from concourse.bass_utils import run_bass_kernel_spmd

F32 = mybir.dt.float32
BF16 = mybir.dt.bfloat16
AF = mybir.ActivationFunctionType
OP = mybir.AluOpType
AX = mybir.AxisListType

T = 2048
C = 1024
NH = 16
NE = 8
DE = 512
NT = T // 128
NQ = T // 512
NC = C // 128
EPS = 1e-5

ASSEMBLY = "opt"
RSTAGE = 9
DEBUG_PHASE = None


class Plan:
    def __init__(self):
        self.ops = {e: [] for e in ("tensor", "vector", "scalar", "sync", "gpsimd")}

    def add(self, eng, fn, waits=()):
        self.ops[eng].append((list(waits), fn))


def emit_block(nc, plan, name=None):
    active = [e for e in plan.ops if plan.ops[e]]
    if not active:
        return
    with nc.Block(name) as block:
        for eng_name in active:
            ops = plan.ops[eng_name]

            def body(eng, ops=ops):
                for waits, fn in ops:
                    for s, v in waits:
                        eng.wait_ge(s, v)
                    fn(eng)

            getattr(block, eng_name)(body)


def build_nc():
    nc = bass.Bass("TRN2", target_bir_lowering=False)

    x_d = nc.dram_tensor("x", [T, C], F32, kind="ExternalInput")
    wt_d = nc.dram_tensor("wt", [C, 3 * C], F32, kind="ExternalInput")
    bqk_d = nc.dram_tensor("bqk", [128, 16], F32, kind="ExternalInput")
    bv_d = nc.dram_tensor("bv", [1, C], F32, kind="ExternalInput")
    w1_d = nc.dram_tensor("w1", [NE, C, DE], F32, kind="ExternalInput")
    b1_d = nc.dram_tensor("b1", [128, NE, 4], F32, kind="ExternalInput")
    w2_d = nc.dram_tensor("w2", [NE, DE, C], F32, kind="ExternalInput")
    b2_d = nc.dram_tensor("b2", [NE, C], F32, kind="ExternalInput")
    cent_d = nc.dram_tensor("cent", [NE, C], F32, kind="ExternalInput")
    wrt_d = nc.dram_tensor("wrt", [C, NE], F32, kind="ExternalInput")
    out_d = nc.dram_tensor("out", [T, C], F32, kind="ExternalOutput")
    dbg_d = {}

    sem_names = [
        "s_t", "s_v", "s_s", "s_vg", "s_g",
        "d_cst", "d_gcst", "d_gdma",
        "d_x0", "d_x1", "d_x2", "d_x3",
        "d_xnT", "d_xn0", "d_xn1", "d_xn2", "d_xn3",
        "d_w10", "d_w11", "d_w20", "d_w21",
        "d_wt0", "d_wt1", "d_wt2", "d_wt3",
        "d_wv",
        "d_y0", "d_y1", "d_y2", "d_y3",
        "d_xf0", "d_xf1",
        "d_ob0", "d_ob1",
        "d_out",
    ]

    es = contextlib.ExitStack()
    with es:
        sem = {n: es.enter_context(nc.semaphore(n)) for n in sem_names}
        CNT = {n: 0 for n in sem_names}

        def mk(fn, semname=None, n=1):
            if semname is None:
                return fn, None
            prev = CNT[semname]
            CNT[semname] += n
            s = sem[semname]
            is_dma = semname.startswith("d_")

            def f(eng, fn=fn, s=s, n=n, prev=prev, is_dma=is_dma):
                if is_dma and prev > 0:
                    eng.wait_ge(s, prev)
                fn(eng).then_inc(s, n)
            return f, CNT[semname]

        def sb(name, shape, dt):
            return es.enter_context(nc.sbuf_tensor(name, shape, dt))

        def phase_fence(p):
            ww = [(sem[n], CNT[n]) for n in sem_names if n.startswith("d_") and CNT[n] > 0]
            for eng in ("tensor", "vector", "scalar", "sync", "gpsimd"):
                p.add(eng, lambda e: e.nop(), waits=list(ww))

        def W(*pairs):
            return [(sem[n], v) for n, v in pairs]

        iota_c = sb("iota_c", [128, 1], F32)
        iota_rb = sb("iota_rb", [128, 512], F32)
        mtri = sb("mtri", [128, 128], BF16)
        idm_bf = sb("idm_bf", [128, 128], BF16)
        idm_f = sb("idm_f", [16, 16], F32)
        ones_bf = sb("ones_bf", [1, 128], BF16)
        bqk_sb = sb("bqk_sb", [128, 16], F32)
        bqkq_sb = sb("bqkq_sb", [128, 8], F32)
        b1_sb = sb("b1_sb", [128, NE, 4], F32)
        bv_sb = sb("bv_sb", [1, C], BF16)
        b2_sb = sb("b2_sb", [NE, C], BF16)
        wcat = sb("wcat", [128, NC, 16], BF16)
        negm = sb("negm", [128, 16], F32)
        rstd_s = sb("rstd_s", [128, NT], F32)
        invn_s = sb("invn_s", [128, NT], F32)
        gates_tm = sb("gates_tm", [128, NT, NE], F32)
        gates_fm = sb("gates_fm", [NE, T], BF16)
        obk = sb("obk", [128, NC, 16], BF16)
        stp_c = sb("stp_c", [128, 1], F32)
        colv = sb("colv", [128, 1], F32)
        rec_sb = sb("rec_sb", [128, 64], F32)
        nmx = sb("nmx", [16, 16], F32)
        negcol = sb("negcol", [16, 1], F32)
        negrow = sb("negrow", [1, 16], F32)
        negrow_bf = sb("negrow_bf", [1, 16], BF16)

        acc = sb("acc", [128, NT, C], BF16)
        xn_fm = sb("xn_fm", [128, NC, T], BF16)

        wtr = sb("wtr", [128, 4, NC, 128], BF16)

        es_w = es.enter_context(contextlib.ExitStack())
        w1sb = es_w.enter_context(nc.sbuf_tensor("w1sb", [128, 2, NC, DE], BF16))
        w2sb = es_w.enter_context(nc.sbuf_tensor("w2sb", [128, 2, 4, C], BF16))
        es_b1 = es_w.enter_context(contextlib.ExitStack())
        x_ring = es_b1.enter_context(nc.sbuf_tensor("x_ring", [128, 11, C], F32))
        xn_tm = es_b1.enter_context(nc.sbuf_tensor("xn_tm", [128, 8, C], BF16))

        def w1_load(ei):
            def f(e, ei=ei):
                return e.dma_start(
                    w1sb[:, ei % 2, :, :],
                    bass.AP(w1_d, ei * C * DE, [[DE, 128], [128 * DE, NC], [1, DE]]))
            return f

        def w2_load(ei):
            def f(e, ei=ei):
                return e.dma_start(
                    w2sb[:, ei % 2, :, :],
                    bass.AP(w2_d, ei * DE * C, [[C, 128], [128 * C, 4], [1, C]]))
            return f

        def wt_load(j):
            def f(e, j=j):
                return e.dma_start(
                    wtr[:, j % 4, :, :],
                    bass.AP(wt_d, j * 128, [[3 * C, 128], [128 * 3 * C, NC], [1, 128]]))
            return f

        def x_load(m):
            def f(e, m=m):
                return e.dma_start(
                    x_ring[:, m % 11, :],
                    bass.AP(x_d, m * 128 * C, [[C, 128], [1, C]]))
            return f

        p = Plan()
        p.add("gpsimd", mk(lambda e: e.iota(iota_c[:, :], [[1, 1]], channel_multiplier=1,
                                            allow_small_or_imprecise_dtypes=True), "d_gcst")[0])
        p.add("gpsimd", mk(lambda e: e.iota(iota_rb[:, :], [[1, 512]], channel_multiplier=0,
                                            allow_small_or_imprecise_dtypes=True), "d_gcst")[0])
        p.add("gpsimd", mk(lambda e: e.memset(acc[:, :, :], 0.0), "d_gcst")[0])
        p.add("gpsimd", mk(lambda e: e.memset(ones_bf[:, :], 1.0), "d_gcst")[0])
        gsetup = CNT["d_gcst"]

        p.add("vector", mk(lambda e: e.tensor_scalar(
            mtri[:, :], iota_rb[:, 0:128], iota_c[:, :], None, OP.is_ge), "s_v")[0],
            waits=W(("d_gcst", gsetup)))
        p.add("vector", mk(lambda e: e.tensor_scalar(
            idm_bf[:, :], iota_rb[:, 0:128], iota_c[:, :], None, OP.is_equal), "s_v")[0])
        p.add("vector", mk(lambda e: e.tensor_scalar(
            idm_f[:, :], iota_rb[0:16, 0:16], iota_c[0:16, :], None, OP.is_equal), "s_v")[0])
        fn, gob = mk(lambda e: e.tensor_scalar(
            stp_c[:, :], iota_c[:, :], 63.5, None, OP.is_gt), "s_vg")
        p.add("vector", fn)
        for cc in range(NC):
            fn, gob = mk(lambda e, cc=cc: e.tensor_scalar(
                colv[:, :], stp_c[:, :], float(2 * cc), None, OP.add), "s_vg")
            p.add("vector", fn, waits=W(("s_vg", gob - 1)))
            fn, gob = mk(lambda e, cc=cc: e.tensor_scalar(
                obk[:, cc, :], iota_rb[:, 0:16], colv[:, :], None, OP.is_equal), "s_vg")
            p.add("vector", fn, waits=W(("s_vg", gob - 1)))
        obk_done = gob

        p.add("sync", mk(lambda e: e.dma_start(bqk_sb[:, :], bqk_d[:, :]), "d_cst", 16)[0])
        p.add("sync", mk(lambda e: e.dma_start(b1_sb[:, :, :], b1_d[:, :, :]), "d_cst", 16)[0])
        cst_b0 = CNT["d_cst"]
        p.add("gpsimd", mk(lambda e: e.dma_start(bv_sb[:, :], bv_d[:, :]), "d_gdma", 16)[0])
        p.add("gpsimd", mk(lambda e: e.dma_start(b2_sb[:, :], b2_d[:, :]), "d_gdma", 16)[0])
        p.add("gpsimd", mk(lambda e: e.dma_start(
            wcat[:, :, 8:16],
            bass.AP(wrt_d, 0, [[8, 128], [1024, 8], [1, 8]])), "d_gdma", 16)[0])

        p.add("vector", mk(lambda e: e.tensor_scalar(
            bqkq_sb[:, :], bqk_sb[:, 0:8], 0.125, None, OP.mult), "s_v")[0],
            waits=W(("d_cst", cst_b0)))

        w1_done, w2_done, wt_done, x_done = {}, {}, {}, {}
        for ei in (0, 1):
            fn, w1_done[ei] = mk(w1_load(ei), f"d_w1{ei % 2}", 16)
            p.add("gpsimd", fn)
            fn, w2_done[ei] = mk(w2_load(ei), f"d_w2{ei % 2}", 16)
            p.add("gpsimd", fn)
        for j in range(4):
            fn, wt_done[j] = mk(wt_load(j), f"d_wt{j % 4}", 16)
            p.add("gpsimd", fn)
        for m in range(11):
            fn, x_done[m] = mk(x_load(m), f"d_x{m % 4}", 16)
            p.add("sync", fn)
        emit_block(nc, p, "b0")

        bn6 = es_b1.enter_context(nc.sbuf_tensor("bn6", [128, 4, 12], F32))
        mvv = es_b1.enter_context(nc.sbuf_tensor("mvv", [128, NT, 2], F32))
        t1g = es_b1.enter_context(nc.sbuf_tensor("t1g", [128, 4], F32))
        t2b = es_b1.enter_context(nc.sbuf_tensor("t2b", [128, NT], F32))
        t3b = es_b1.enter_context(nc.sbuf_tensor("t3b", [128, NT], F32))
        nsq = es_b1.enter_context(nc.sbuf_tensor("nsq", [128, NT], F32))
        with contextlib.ExitStack() as es2:
            def sb2(name, shape, dt):
                return es2.enter_context(nc.sbuf_tensor(name, shape, dt))
            cent_sb = sb2("cent_sb", [NE, C], F32)
            cn_bf = sb2("cn_bf", [NE, C], BF16)
            csq = sb2("csq", [NE, C], F32)
            cno = sb2("cno", [NE, 1], F32)
            gates_bf = sb2("gates_bf", [128, NT, NE], BF16)
            lg8 = sb2("lg8", [128, NE], F32)
            lgm = sb2("lgm", [128, NE], F32)
            lgs = sb2("lgs", [128, NT, NE], F32)
            lgk = sb2("lgk", [128, NT, NE], F32)
            m1c = sb2("m1c", [128, 1], F32)
            m2c = sb2("m2c", [128, 1], F32)
            ssum = sb2("ssum", [128, 1], F32)
            hsb = sb2("hsb", [128, 4, T], BF16)
            tps = es2.enter_context(nc.psum_tensor("tps", [128, 1024], BF16))
            hps = [es2.enter_context(nc.psum_tensor(f"hps{i}", [128, 512], F32)) for i in range(2)]
            ops_ = [es2.enter_context(nc.psum_tensor(f"ops{i}", [128, 512], F32)) for i in range(2)]
            b2ps = [es2.enter_context(nc.psum_tensor(f"b2ps{i}", [128, 512], F32)) for i in range(2)]
            lgps = b2ps

            SEG = {}
            class SegP:
                def __init__(self, name):
                    self.ops = {e: [] for e in ("tensor", "vector", "scalar", "sync", "gpsimd")}
                    SEG[name] = self
                def add(self, eng, fn, waits=()):
                    self.ops[eng].append((list(waits), fn))
            s_cn = SegP("cn")
            s_ln = [SegP(f"ln{m}") for m in range(NT)]
            s_lnB = [SegP(f"lnB{m}") for m in range(NT)]
            s_rA = [SegP(f"rA{m}") for m in range(NT)]
            s_rB = [SegP(f"rB{m}") for m in range(NT)]
            s_rC = [SegP(f"rC{m}") for m in range(NT)]
            s_ffn0 = SegP("ffn0")
            s_ffn1 = SegP("ffn1")

            p = s_cn
            p.add("sync", mk(lambda e: e.dma_start(cent_sb[:, :], cent_d[:, :]), "d_cst", 16)[0])
            cst_now = CNT["d_cst"]
            fn, g0 = mk(lambda e: e.tensor_tensor(csq[:, :], cent_sb[:, :], cent_sb[:, :], OP.mult), "s_vg")
            p.add("vector", fn, waits=W(("d_cst", cst_now)))
            fn, g1 = mk(lambda e: e.reduce_sum(cno[:, :], csq[:, :], AX.X), "s_vg")
            p.add("vector", fn, waits=W(("s_vg", g0)))
            fn, c_cno = mk(lambda e: e.nop(), "s_v")
            p.add("vector", fn, waits=W(("s_vg", g1)))
            fn, c_csqr = mk(lambda e: e.activation(cno[:, :], cno[:, :], AF.Sqrt), "s_s")
            p.add("scalar", fn, waits=W(("s_v", c_cno)))
            fn, g2 = mk(lambda e: e.tensor_scalar(cno[:, :], cno[:, :], 1e-12, None, OP.max), "s_vg")
            p.add("vector", fn, waits=W(("s_s", c_csqr)))
            fn, g3 = mk(lambda e: e.reciprocal(cno[:, :], cno[:, :]), "s_vg")
            p.add("vector", fn, waits=W(("s_vg", g2)))
            fn, c_cn = mk(lambda e: e.tensor_scalar(
                cn_bf[:, :], cent_sb[:, :], cno[:, :], None, OP.mult), "s_v")
            p.add("vector", fn, waits=W(("s_vg", g3)))
            ev_cn = {}
            for c in range(NC):
                w = W(("s_v", c_cn))
                if c > 0:
                    w += W(("s_s", ev_cn[c - 1]))
                fn, c_tp = mk(lambda e, c=c: e.transpose(
                    tps[:, 0:8], cn_bf[0:8, 128 * c:128 * (c + 1)], idm_bf[0:8, 0:8]), "s_t")
                p.add("tensor", fn, waits=w)
                fn, ev_cn[c] = mk(lambda e, c=c: e.activation(
                    wcat[:, c, 0:8], tps[:, 0:8], AF.Identity, bias=0.0, scale=1.0), "s_s")
                p.add("scalar", fn, waits=W(("s_t", c_tp)))

            xn_done = {}
            xn_t_done = {}
            c_rstdg, c_nsqg = {}, {}
            ga_fence = {}
            gb_fence = {}

            def var_ap(g):
                return bass.AP(mvv, 8 * g + 1, [[NT * 2, 128], [2, 4]])

            def plan_lnA(g):
                p = s_ln[g]
                gaggr = None
                for i in range(4):
                    m = 4 * g + i
                    w0 = W((f"d_x{m % 4}", x_done[m]))
                    if i == 0 and g >= 1:
                        w0 += W(("s_vg", ga_fence[g - 1]))
                    p.add("vector", mk(lambda e, i=i, m=m: e.bn_stats(
                        bn6[:, i, 0:6], x_ring[:, m % 11, 0:512]))[0], waits=w0)
                    fn, gs = mk(lambda e, i=i, m=m: e.bn_stats(
                        bn6[:, i, 6:12], x_ring[:, m % 11, 512:1024]), "s_vg")
                    p.add("vector", fn)
                for i in range(4):
                    m = 4 * g + i
                    fn, gaggr = mk(lambda e, i=i, m=m: e.bn_aggr(
                        mvv[:, m, :], bn6[:, i, :]), "s_vg")
                    p.add("vector", fn, waits=W(("s_vg", gs)) if i == 0 else ())
                fn, g1 = mk(lambda e, g=g: e.tensor_scalar(
                    t1g[:, :], var_ap(g), EPS, None, OP.add), "s_vg")
                wg1 = W(("s_vg", gaggr))
                if g >= 2:
                    wg1 += W(("s_vg", gb_fence[g - 2]))
                p.add("vector", fn, waits=wg1)
                fn, c_rvar4 = mk(lambda e, g=g: e.reciprocal(
                    t2b[:, 4 * g:4 * g + 4], t1g[:, :]), "s_v")
                p.add("vector", fn, waits=W(("s_vg", g1)))
                fn, g2 = mk(lambda e, g=g: e.scalar_tensor_tensor(
                    t3b[:, 4 * g:4 * g + 4], var_ap(g), float(C), t2b[:, 4 * g:4 * g + 4],
                    OP.mult, OP.mult), "s_vg")
                p.add("vector", fn, waits=W(("s_v", c_rvar4)))
                ga_fence[g] = g2
                fn, c_t3 = mk(lambda e: e.nop(), "s_v")
                p.add("vector", fn, waits=W(("s_vg", g2)))
                fn, c_rstdg[g] = mk(lambda e, g=g: e.activation(
                    rstd_s[:, 4 * g:4 * g + 4], t2b[:, 4 * g:4 * g + 4], AF.Sqrt), "s_s")
                p.add("scalar", fn, waits=W(("s_v", c_rvar4)))
                fn, c_nsqg[g] = mk(lambda e, g=g: e.activation(
                    nsq[:, 4 * g:4 * g + 4], t3b[:, 4 * g:4 * g + 4], AF.Sqrt), "s_s")
                p.add("scalar", fn, waits=W(("s_v", c_t3)))

            def plan_lnB(g):
                p = s_lnB[g]
                for i in range(4):
                    m = 4 * g + i
                    wxn = W(("s_s", c_rstdg[g])) if i == 0 else []
                    if m >= 8:
                        wxn = list(wxn) + [(sem[f"d_xn{m % 4}"], xn_t_done[m - 8])]
                    fn, c_xn = mk(lambda e, m=m: e.tensor_scalar(
                        xn_tm[:, m % 8, :], x_ring[:, m % 11, :], mvv[:, m, 0:1], rstd_s[:, m:m + 1],
                        OP.subtract, OP.mult), "s_v")
                    p.add("vector", fn, waits=wxn)
                    xn_done[m] = c_xn
                    if m + 11 < NT:
                        fn, x_done[m + 11] = mk(x_load(m + 11), f"d_x{(m + 11) % 4}", 16)
                        p.add("sync", fn, waits=W(("s_v", c_xn)))
                    fn, xn_t_done[m] = mk(lambda e, m=m: e.dma_start(
                        xn_fm[:, :, 128 * m:128 * (m + 1)], xn_tm[:, m % 8, :], transpose=True),
                        f"d_xn{m % 4}", 16)
                    p.add("sync", fn, waits=W(("s_v", c_xn)))
                w3 = W(("s_s", c_nsqg[g]))
                if g >= 1:
                    w3 += W(("s_vg", gb_fence[g - 1]))
                w3 += W(("s_vg", ga_fence[min(g + 1, 3)]))
                fn, g3 = mk(lambda e, g=g: e.tensor_scalar(
                    t1g[:, :], nsq[:, 4 * g:4 * g + 4], 1e-12, None, OP.max), "s_vg")
                p.add("vector", fn, waits=w3)
                fn, g4 = mk(lambda e, g=g: e.reciprocal(
                    invn_s[:, 4 * g:4 * g + 4], t1g[:, :]), "s_vg")
                p.add("vector", fn, waits=W(("s_vg", g3)))
                gb_fence[g] = g4

            ln_order = []
            for seq in (("A", 0), ("A", 1), ("B", 0), ("A", 2), ("B", 1), ("A", 3), ("B", 2), ("B", 3)):
                kind, g = seq
                if kind == "A":
                    plan_lnA(g)
                    ln_order.append(s_ln[g])
                else:
                    plan_lnB(g)
                    ln_order.append(s_lnB[g])

            def xn_wait_tile(m):
                return [(sem[f"d_xn{m % 4}"], xn_t_done[m])]

            def xn_wait_tt(tt):
                ww = []
                for i in range(4):
                    ww += xn_wait_tile(4 * tt + i)
                return ww

            XNALL = [(sem[f"d_xn{i}"], xn_t_done[12 + i]) for i in range(4)]

            gcst_all = CNT["d_gdma"]
            lg_read = {}
            ra_fence = {}
            rb_fence = {}
            gfm_ev = {}
            c_gbf = {}
            c_sub = {}
            exp_rt = {}
            for m in range(NT if ASSEMBLY not in ("norouter", "lnonly") else 0):
                p = s_rA[m]
                bi = m % 2
                w = xn_wait_tile(m) + W(("s_s", ev_cn[NC - 1]), ("d_gdma", gcst_all))
                if m >= 2 and RSTAGE >= 1:
                    w += W(("s_v", lg_read[m - 2]))
                for c in range(NC):
                    fn, c_mm = mk(lambda e, m=m, c=c, bi=bi: e.matmul(
                        lgps[bi][:, 0:16], xn_fm[:, c, 128 * m:128 * (m + 1)], wcat[:, c, :],
                        start=(c == 0), stop=(c == NC - 1)), "s_t")
                    p.add("tensor", fn, waits=w if c == 0 else ())
                if RSTAGE < 1:
                    continue
                fn, g0 = mk(lambda e, m=m, bi=bi: e.tensor_scalar(
                    lg8[:, :], lgps[bi][:, 0:8], invn_s[:, m:m + 1], None, OP.mult), "s_vg")
                wg0 = W(("s_t", c_mm))
                if m >= 1:
                    wg0 += W(("s_vg", ra_fence[m - 1]))
                p.add("vector", fn, waits=wg0)
                fn, g1 = mk(lambda e, bi=bi: e.tensor_tensor(
                    lg8[:, :], lg8[:, :], lgps[bi][:, 8:16], OP.add), "s_vg")
                p.add("vector", fn, waits=W(("s_vg", g0)))
                fn, lg_read[m] = mk(lambda e: e.nop(), "s_v")
                p.add("vector", fn, waits=W(("s_vg", g1)))
                if RSTAGE < 2:
                    continue
                fn, g2 = mk(lambda e: e.reduce_max(m1c[:, :], lg8[:, :], AX.X), "s_vg")
                p.add("vector", fn)
                fn, g3 = mk(lambda e: e.tensor_scalar(
                    lgm[:, :], lg8[:, :], m1c[:, :], None, OP.is_ge), "s_vg")
                p.add("vector", fn, waits=W(("s_vg", g2)))
                fn, g4 = mk(lambda e: e.scalar_tensor_tensor(
                    lgm[:, :], lgm[:, :], -1e30, lg8[:, :], OP.mult, OP.add), "s_vg")
                p.add("vector", fn, waits=W(("s_vg", g3)))
                fn, g5 = mk(lambda e: e.reduce_max(m2c[:, :], lgm[:, :], AX.X), "s_vg")
                p.add("vector", fn, waits=W(("s_vg", g4)))
                fn, g6 = mk(lambda e, m=m: e.tensor_scalar(
                    lgs[:, m, :], lg8[:, :], m1c[:, :], None, OP.subtract), "s_vg")
                p.add("vector", fn, waits=W(("s_vg", g5)))
                fn, c_sub[m] = mk(lambda e: e.nop(), "s_v")
                p.add("vector", fn, waits=W(("s_vg", g6)))
                fn, g7 = mk(lambda e, m=m: e.tensor_scalar(
                    lgk[:, m, :], lg8[:, :], m2c[:, :], None, OP.is_ge), "s_vg")
                p.add("vector", fn)
                g7_rt = g7
                ra_fence[m] = g7
                fn, exp_rt[m] = mk(lambda e, m=m: e.activation(lgs[:, m, :], lgs[:, m, :], AF.Exp), "s_s")
                s_rA[m].add("scalar", fn, waits=W(("s_v", c_sub[m])))
            ra_g7 = dict(ra_fence)
            for m in range(NT if (ASSEMBLY not in ("norouter", "lnonly") and RSTAGE >= 3) else 0):
                p = s_rB[m]
                fn, g8 = mk(lambda e, m=m: e.tensor_tensor(
                    lgk[:, m, :], lgk[:, m, :], lgs[:, m, :], OP.mult), "s_vg")
                wg8 = W(("s_s", exp_rt[m]), ("s_vg", ra_g7[m]))
                if m >= 1 and (m - 1) in rb_fence:
                    wg8 += W(("s_vg", rb_fence[m - 1]))
                p.add("vector", fn, waits=wg8)
                if RSTAGE < 3.2:
                    continue
                fn, g9 = mk(lambda e, m=m: e.reduce_sum(ssum[:, :], lgk[:, m, :], AX.X), "s_vg")
                p.add("vector", fn, waits=W(("s_vg", g8)))
                if RSTAGE < 3.4:
                    continue
                fn, g10 = mk(lambda e: e.reciprocal(ssum[:, :], ssum[:, :]), "s_vg")
                p.add("vector", fn, waits=W(("s_vg", g9)))
                if RSTAGE < 3.8:
                    continue
                fn, g11 = mk(lambda e, m=m: e.tensor_scalar(
                    gates_tm[:, m, :], lgk[:, m, :], ssum[:, :], None, OP.mult), "s_vg")
                p.add("vector", fn, waits=W(("s_vg", g10)))
                rb_fence[m] = g11
                fn, c_gbf[m] = mk(lambda e, m=m: e.tensor_copy(
                    gates_bf[:, m, :], gates_tm[:, m, :]), "s_v")
                p.add("vector", fn, waits=W(("s_vg", g11)))
            c_gel, c_stt = {}, {}
            c_hmm_all, c_omm_all = {}, {}
            ffn_experts = list(range(NE if ASSEMBLY not in ("noffn", "seqnoffn", "lnonly") else 0))
            def plan_expert(ei, p):
                sl = ei % 2
                for g in range(16):
                    hc, tt = divmod(g, 4)
                    bi = g % 2
                    w = []
                    if g == 0:
                        w += W((f"d_w1{sl}", w1_done[ei]))
                    if ei == 0:
                        w += xn_wait_tt(tt)
                    if (ei, g - 2) in c_gel:
                        w += W(("s_s", c_gel[(ei, g - 2)]))
                    elif ei > 0 and g < 2:
                        w += W(("s_s", c_gel[(ei - 1, 14 + g)]))
                    for c in range(NC):
                        fn, c_mm = mk(lambda e, sl=sl, hc=hc, tt=tt, bi=bi, c=c: e.matmul(
                            hps[bi][:, :], w1sb[:, sl, c, 128 * hc:128 * (hc + 1)],
                            xn_fm[:, c, 512 * tt:512 * (tt + 1)],
                            start=(c == 0), stop=(c == NC - 1)), "s_t")
                        p.add("tensor", fn, waits=w if c == 0 else ())
                    if g == 15:
                        c_hmm_all[ei] = c_mm
                    w2w = W(("s_t", c_mm))
                    if ei >= 1 and g == 0:
                        w2w += W(("s_t", c_omm_all[ei - 1]))
                    fn, c_gel[(ei, g)] = mk(lambda e, ei=ei, hc=hc, tt=tt, bi=bi: e.activation(
                        hsb[:, hc, 512 * tt:512 * (tt + 1)], hps[bi][:, :], AF.Gelu,
                        bias=b1_sb[:, ei, hc:hc + 1], scale=1.0), "s_s")
                    p.add("scalar", fn, waits=w2w)
                if ei + 2 < NE:
                    fn, w1_done[ei + 2] = mk(w1_load(ei + 2), f"d_w1{ei % 2}", 16)
                    p.add("gpsimd", fn, waits=W(("s_t", c_hmm_all[ei])))
                for og in range(32):
                    t2, ch = divmod(og, 2)
                    bi = og % 2
                    w = []
                    if og == 0:
                        w += W((f"d_w2{sl}", w2_done[ei]), ("s_s", c_gel[(ei, 15)]))
                    if (ei, og - 2) in c_stt:
                        w += W(("s_v", c_stt[(ei, og - 2)]))
                    elif ei > 0 and og < 2:
                        w += W(("s_v", c_stt[(ei - 1, 30 + og)]))
                    for hc in range(4):
                        fn, c_mm = mk(lambda e, t2=t2, ch=ch, bi=bi, hc=hc, sl=sl: e.matmul(
                            ops_[bi][:, :], hsb[:, hc, 128 * t2:128 * (t2 + 1)],
                            w2sb[:, sl, hc, 512 * ch:512 * (ch + 1)],
                            start=(hc == 0), stop=(hc == 3)), "s_t")
                        p.add("tensor", fn, waits=w if hc == 0 else ())
                    if og == 31:
                        c_omm_all[ei] = c_mm
                    fn, c_stt[(ei, og)] = mk(lambda e, ei=ei, t2=t2, ch=ch, bi=bi: e.scalar_tensor_tensor(
                        acc[:, t2, 512 * ch:512 * (ch + 1)], ops_[bi][:, :],
                        gates_tm[:, t2, ei:ei + 1], acc[:, t2, 512 * ch:512 * (ch + 1)],
                        OP.mult, OP.add), "s_v")
                    stt_w = W(("s_t", c_mm))
                    if ei == 0 and og == 0:
                        stt_w += W(("d_gcst", gsetup))
                    p.add("vector", fn, waits=stt_w)
                if ei + 2 < NE:
                    fn, w2_done[ei + 2] = mk(w2_load(ei + 2), f"d_w2{ei % 2}", 16)
                    p.add("gpsimd", fn, waits=W(("s_t", c_omm_all[ei])))
            for ei in ffn_experts[:1]:
                plan_expert(ei, s_ffn0)
            c_gtp = {}
            if RSTAGE < 4:
                pass
            for m in range(NT if (ASSEMBLY not in ("norouter", "lnonly") and RSTAGE >= 4) else 0):
                p = s_rC[m]
                w = W(("s_v", c_gbf[m]))
                if m >= 1:
                    w += W(("s_v", gfm_ev[m - 1]))
                if m == 0:
                    w += W(("s_s", ev_cn[NC - 1]))
                fn, c_gtp[m] = mk(lambda e, m=m: e.transpose(
                    tps[0:8, 128:256], gates_bf[:, m, :], idm_bf[:, :]), "s_t")
                p.add("tensor", fn, waits=w)
                fn, gfm_ev[m] = mk(lambda e, m=m: e.tensor_copy(
                    gates_fm[:, 128 * m:128 * (m + 1)], tps[0:8, 128:256]), "s_v")
                p.add("vector", fn, waits=W(("s_t", c_gtp[m])))
            for ei in ffn_experts[1:]:
                plan_expert(ei, s_ffn1)
            p = s_ffn1
            c_b2ev = {}
            for og in range(32 if ASSEMBLY not in ("noffn", "seqnoffn", "norouter", "lnonly") else 0):
                t2, ch = divmod(og, 2)
                bi2 = og % 2
                w = []
                if og == 0:
                    w += W(("d_gdma", gcst_all), ("s_v", lg_read[NT - 1]),
                           ("s_v", gfm_ev[NT - 1]))
                if og >= 2:
                    w += W(("s_v", c_b2ev[og - 2]))
                fn, c_mm = mk(lambda e, t2=t2, ch=ch, bi2=bi2: e.matmul(
                    b2ps[bi2][:, :], gates_fm[0:8, 128 * t2:128 * (t2 + 1)],
                    b2_sb[0:8, 512 * ch:512 * (ch + 1)], start=True, stop=True), "s_t")
                p.add("tensor", fn, waits=w)
                fn, c_b2ev[og] = mk(lambda e, t2=t2, ch=ch, bi2=bi2: e.tensor_tensor(
                    acc[:, t2, 512 * ch:512 * (ch + 1)], acc[:, t2, 512 * ch:512 * (ch + 1)],
                    b2ps[bi2][:, :], OP.add), "s_v")
                p.add("vector", fn, waits=W(("s_t", c_mm)))

            p = Plan()
            def cat(eng, *segs):
                for s in segs:
                    p.ops[eng].extend(s.ops[eng])
            cat("sync", s_cn, *ln_order)
            cat("gpsimd", s_ffn0, s_ffn1)
            cat("tensor", s_cn, *s_rA, s_ffn0, *s_rC, s_ffn1)
            cat("vector", s_cn, *ln_order, *s_rA, *s_rB, s_ffn0, *s_rC, s_ffn1)
            cat("scalar", s_cn, *ln_order, *s_rA, s_ffn0, s_ffn1)
            emit_block(nc, p, "b123")
        es_b1.close()
        es_w.close()

        def finish_dbg(p):
            p.ops["sync"].append((W(("d_out", CNT["d_out"])), lambda e: e.nop()))
            emit_block(nc, p, "dbg")
            return nc, dbg_d

        if DEBUG_PHASE == "b123":
            dbg_d["dbg_acc"] = nc.dram_tensor("dbg_acc", [128, NT, C], BF16, kind="ExternalOutput")
            p = Plan()
            p.add("sync", mk(lambda e: e.dma_start(
                dbg_d["dbg_acc"][:, :, :], acc[:, :, :]), "d_out", 16)[0],
                waits=W(("s_v", CNT["s_v"]), ("s_s", CNT["s_s"]), ("s_t", CNT["s_t"])))
            return finish_dbg(p)

        es_qkv = es.enter_context(contextlib.ExitStack())
        q_fm = es_qkv.enter_context(nc.sbuf_tensor("q_fm", [128, NC, T], BF16))
        k_fm = es_qkv.enter_context(nc.sbuf_tensor("k_fm", [128, NC, T], BF16))
        v_sb = es_qkv.enter_context(nc.sbuf_tensor("v_sb", [128, NT, NH * 65], BF16))

        with contextlib.ExitStack() as es4:
            qps = [es4.enter_context(nc.psum_tensor(f"qps{i}", [128, 512], F32)) for i in range(4)]
            nps = es4.enter_context(nc.psum_tensor("nps", [128, 16], F32))
            sqb = es4.enter_context(nc.sbuf_tensor("sqb", [128, 2, T], BF16))

            p = Plan()
            phase_fence(p)
            p.add("gpsimd", mk(lambda e: e.memset(
                bass.AP(v_sb, 64, [[NT * NH * 65, 128], [NH * 65, NT], [65, NH]]), 1.0),
                "d_gcst")[0])
            vones = CNT["d_gcst"]

            c_qkev = {}
            gidx = 0
            for j in range(16):
                slot = j % 4
                for tt in range(4):
                    bi = gidx % 4
                    w = []
                    if tt == 0:
                        w += W((f"d_wt{slot}", wt_done[j]))
                        if j == 0:
                            w += XNALL
                    if gidx - 4 >= 0:
                        w += W(("s_s", c_qkev[gidx - 4]))
                    for c in range(NC):
                        fn, c_mm = mk(lambda e, slot=slot, c=c, tt=tt, bi=bi: e.matmul(
                            qps[bi][:, :], wtr[:, slot, c, :], xn_fm[:, c, 512 * tt:512 * (tt + 1)],
                            start=(c == 0), stop=(c == NC - 1)), "s_t")
                        p.add("tensor", fn, waits=w if c == 0 else ())
                    if j < 8:
                        fn, c_qkev[gidx] = mk(lambda e, j=j, tt=tt, bi=bi: e.activation(
                            q_fm[:, j, 512 * tt:512 * (tt + 1)], qps[bi][:, :], AF.Identity,
                            bias=bqkq_sb[:, j:j + 1], scale=0.125), "s_s")
                    else:
                        fn, c_qkev[gidx] = mk(lambda e, j=j, tt=tt, bi=bi: e.activation(
                            k_fm[:, j - 8, 512 * tt:512 * (tt + 1)], qps[bi][:, :], AF.Identity,
                            bias=bqk_sb[:, j:j + 1], scale=1.0), "s_s")
                    p.add("scalar", fn, waits=W(("s_t", c_mm)))
                    gidx += 1
                if j + 4 < 20:
                    fn, wt_done[j + 4] = mk(wt_load(j + 4), f"d_wt{(j + 4) % 4}", 16)
                    p.add("gpsimd", fn, waits=W(("s_t", c_mm)))
            c_vev = {}
            for vh in range(2):
                for t2 in range(NT):
                    vgi = 16 * vh + t2
                    bi = vgi % 4
                    w = []
                    if t2 == 0:
                        for jj in range(16 + 4 * vh, 20 + 4 * vh):
                            w += W((f"d_wt{jj % 4}", wt_done[jj]))
                        if vh == 0:
                            w += W(("d_gcst", vones))
                    if vgi - 4 >= 0:
                        w += W(("s_s", c_vev[vgi - 4]))
                    else:
                        w += W(("s_s", c_qkev[60 + vgi]))
                    for c in range(NC):
                        fn, c_mm = mk(lambda e, t2=t2, bi=bi, c=c: e.matmul(
                            qps[bi][:, :], xn_fm[:, c, 128 * t2:128 * (t2 + 1)],
                            bass.AP(wtr, c * 128, [[4 * NC * 128, 128], [NC * 128, 4], [1, 128]]),
                            start=(c == 0), stop=False), "s_t")
                        p.add("tensor", fn, waits=w if c == 0 else ())
                    fn, c_mm = mk(lambda e, vh=vh, bi=bi: e.matmul(
                        qps[bi][:, :], ones_bf[:, :], bv_sb[:, 512 * vh:512 * (vh + 1)],
                        start=False, stop=True), "s_t")
                    p.add("tensor", fn)
                    fn, c_vev[vgi] = mk(lambda e, t2=t2, vh=vh, bi=bi: e.activation(
                        bass.AP(v_sb, t2 * NH * 65 + vh * 8 * 65,
                                [[NT * NH * 65, 128], [65, 8], [1, 64]]),
                        qps[bi][:, :], AF.Identity, bias=0.0, scale=1.0), "s_s")
                    p.add("scalar", fn, waits=W(("s_t", c_mm)))
                if vh == 0:
                    for jj in range(20, 24):
                        fn, wt_done[jj] = mk(wt_load(jj), f"d_wt{jj % 4}", 16)
                        p.add("gpsimd", fn, waits=W(("s_t", c_mm)))
            norm_section = DEBUG_PHASE != "qkv0"
            if norm_section:
                c_sq = {}
                c_nmm = {}
                c_nred = {}
                last_vev = c_vev[31]
                for si, srcb in enumerate((q_fm, k_fm)):
                    for cc in range(NC):
                        sl2 = (si * NC + cc) % 2
                        w = []
                        if si == 0 and cc == 0:
                            w += W(("s_s", c_qkev[63]), ("s_vg", obk_done))
                        if (si * NC + cc) >= 2:
                            w += W(("s_t", c_nmm[si * NC + cc - 2]))
                        fn, c_sq[si * NC + cc] = mk(lambda e, srcb=srcb, cc=cc, sl2=sl2: e.tensor_tensor(
                            sqb[:, sl2, :], srcb[:, cc, :], srcb[:, cc, :], OP.mult), "s_v")
                        p.add("vector", fn, waits=w)
                        for tt in range(4):
                            wmm = []
                            if cc == 0 and tt == 0:
                                wmm += W(("s_v", c_sq[si * NC + cc]))
                                if si == 0:
                                    wmm += W(("s_s", last_vev))
                                else:
                                    wmm += W(("s_v", c_nred[3]))
                            elif tt == 0:
                                wmm += W(("s_v", c_sq[si * NC + cc]))
                            fn, c_mm2 = mk(lambda e, cc=cc, tt=tt, sl2=sl2: e.matmul(
                                qps[tt][0:16, :], obk[:, cc, :], sqb[:, sl2, 512 * tt:512 * (tt + 1)],
                                start=(cc == 0), stop=(cc == NC - 1)), "s_t")
                            p.add("tensor", fn, waits=wmm)
                        c_nmm[si * NC + cc] = c_mm2
                    gv0 = None
                    for tt in range(4):
                        fn, gv0 = mk(lambda e, si=si, tt=tt: e.reduce_max(
                            nmx[:, 8 * si + tt:8 * si + tt + 1], qps[tt][0:16, :], AX.X), "s_vg")
                        p.add("vector", fn, waits=W(("s_t", c_nmm[si * NC + NC - 1])))
                        c_nred[tt] = gv0 if si == 0 else c_nred[tt]
                    fn, gv0 = mk(lambda e, si=si: e.tensor_tensor(
                        nmx[:, 8 * si:8 * si + 1], nmx[:, 8 * si:8 * si + 1],
                        nmx[:, 8 * si + 1:8 * si + 2], OP.max), "s_vg")
                    p.add("vector", fn, waits=W(("s_vg", gv0 - 1)))
                    fn, gv0 = mk(lambda e, si=si: e.tensor_tensor(
                        nmx[:, 8 * si + 2:8 * si + 3], nmx[:, 8 * si + 2:8 * si + 3],
                        nmx[:, 8 * si + 3:8 * si + 4], OP.max), "s_vg")
                    p.add("vector", fn, waits=W(("s_vg", gv0 - 1)))
                    fn, gv0 = mk(lambda e, si=si: e.tensor_tensor(
                        nmx[:, 8 * si:8 * si + 1], nmx[:, 8 * si:8 * si + 1],
                        nmx[:, 8 * si + 2:8 * si + 3], OP.max), "s_vg")
                    p.add("vector", fn, waits=W(("s_vg", gv0 - 1)))
                    fn, c_vmax = mk(lambda e: e.nop(), "s_v")
                    p.add("vector", fn, waits=W(("s_vg", gv0)))
                    if si == 0:
                        c_nred[3] = c_vmax
                fn, c_sq1 = mk(lambda e: e.activation(
                    nmx[0:16, 4:5], nmx[0:16, 0:1], AF.Sqrt), "s_s")
                p.add("scalar", fn, waits=W(("s_v", c_vmax)))
                fn, c_sq2 = mk(lambda e: e.activation(
                    nmx[0:16, 5:6], nmx[0:16, 8:9], AF.Sqrt), "s_s")
                p.add("scalar", fn)
                fn, gn = mk(lambda e: e.tensor_tensor(
                    negcol[:, :], nmx[0:16, 4:5], nmx[0:16, 5:6], OP.mult), "s_vg")
                p.add("vector", fn, waits=W(("s_s", c_sq2)))
                fn, c_nc = mk(lambda e: e.tensor_scalar(
                    negcol[:, :], negcol[:, :], -1.0, None, OP.mult), "s_v")
                p.add("vector", fn, waits=W(("s_vg", gn)))
                fn, c_ntp = mk(lambda e: e.transpose(
                    nps[0:1, 0:16], negcol[:, :], idm_f[:, :]), "s_t")
                p.add("tensor", fn, waits=W(("s_v", c_nc)))
                fn, gn = mk(lambda e: e.tensor_copy(negrow[:, :], nps[0:1, 0:16]), "s_vg")
                p.add("vector", fn, waits=W(("s_t", c_ntp)))
                fn, c_nr = mk(lambda e: e.tensor_copy(negrow_bf[:, :], negrow[:, :]), "s_v")
                p.add("vector", fn, waits=W(("s_vg", gn)))
                fn, c_nmmb = mk(lambda e: e.matmul(
                    nps[:, :], ones_bf[:, :], negrow_bf[:, :], start=True, stop=True), "s_t")
                p.add("tensor", fn, waits=W(("s_v", c_nr)))
                fn, c_nev = mk(lambda e: e.tensor_copy(negm[:, :], nps[:, :]), "s_v")
                p.add("vector", fn, waits=W(("s_t", c_nmmb)))
            emit_block(nc, p, "b4_qkv")

        if DEBUG_PHASE in ("qkv", "qkv0"):
            dbg_d["dbg_q"] = nc.dram_tensor("dbg_q", [128, NC, T], BF16, kind="ExternalOutput")
            dbg_d["dbg_k"] = nc.dram_tensor("dbg_k", [128, NC, T], BF16, kind="ExternalOutput")
            dbg_d["dbg_v"] = nc.dram_tensor("dbg_v", [128, NT, NH * 65], BF16, kind="ExternalOutput")
            dbg_d["dbg_negm"] = nc.dram_tensor("dbg_negm", [128, 16], F32, kind="ExternalOutput")
            p = Plan()
            p.add("sync", mk(lambda e: e.dma_start(dbg_d["dbg_q"][:, :, :], q_fm[:, :, :]), "d_out", 16)[0])
            p.add("sync", mk(lambda e: e.dma_start(dbg_d["dbg_k"][:, :, :], k_fm[:, :, :]), "d_out", 16)[0])
            p.add("sync", mk(lambda e: e.dma_start(dbg_d["dbg_v"][:, :, :], v_sb[:, :, :]), "d_out", 16)[0])
            p.add("sync", mk(lambda e: e.dma_start(dbg_d["dbg_negm"][:, :], negm[:, :]), "d_out", 16)[0])
            return finish_dbg(p)

        with contextlib.ExitStack() as es5:
            pt = es5.enter_context(nc.sbuf_tensor("pt", [128, 4, 512], BF16))
            ystage = es5.enter_context(nc.sbuf_tensor("ystage", [80, 4, 512], BF16))
            yqt = es5.enter_context(nc.sbuf_tensor("yqt", [128, 4, NH * 80], BF16))
            xf = es5.enter_context(nc.sbuf_tensor("xf", [128, 2, C], F32))
            sps = [es5.enter_context(nc.psum_tensor(f"sps{i}", [128, 512], F32)) for i in range(4)]
            pvp = [es5.enter_context(nc.psum_tensor(f"pvp{i}", [65, 512], F32)) for i in range(4)]

            p = Plan()
            phase_fence(p)
            fn, c_ystz = mk(lambda e: e.memset(ystage[64:80, :, :], 0.0), "s_v")
            p.add("vector", fn)
            tj = 0
            exp_cnt, msk_cnt, pv_cnt = {}, {}, {}
            yev_cnt = {}
            yT_uses = {0: 0, 1: 0, 2: 0, 3: 0}
            acc_cnt = {}
            add_cnt = {}

            for qt in range(NQ):
                nkb = 4 * qt + 4
                for hp in range(8):
                    ghA = 16 * qt + 2 * hp
                    hA, hB = 2 * hp, 2 * hp + 1
                    gpar = (8 * qt + hp) % 2
                    pvA, pvB = pvp[2 * gpar], pvp[2 * gpar + 1]
                    sc_of = {}
                    for kb in range(nkb):
                        j = kb - 4 * qt
                        n0 = 128 * j if j >= 0 else 0
                        biA = sps[2 * (tj % 2)]
                        biB = sps[2 * (tj % 2) + 1]
                        w = []
                        if tj >= 2:
                            w += W(("s_s", exp_cnt[2 * (tj - 2) + 1]))
                        fn, _ = mk(lambda e, hp=hp, kb=kb, qt=qt, n0=n0, biA=biA: e.matmul(
                            biA[:, n0:512], k_fm[0:64, hp, 128 * kb:128 * (kb + 1)],
                            q_fm[0:64, hp, 512 * qt + n0:512 * (qt + 1)],
                            start=True, stop=True), "s_t")
                        p.add("tensor", fn, waits=w)
                        fn, cB = mk(lambda e, hp=hp, kb=kb, qt=qt, n0=n0, biB=biB: e.matmul(
                            biB[:, n0:512], k_fm[64:128, hp, 128 * kb:128 * (kb + 1)],
                            q_fm[64:128, hp, 512 * qt + n0:512 * (qt + 1)],
                            start=True, stop=True, tile_position=(64, 0)), "s_t")
                        p.add("tensor", fn)
                        sc_of[kb] = (tj, n0)
                        for which, (bank, head) in enumerate(((biA, hA), (biB, hB))):
                            ht = 2 * tj + which
                            slot = ht % 4
                            wS = W(("s_t", cB))
                            if ht - 4 >= 0:
                                wS += W(("s_t", pv_cnt[ht - 4]))
                            fn, exp_cnt[ht] = mk(lambda e, bank=bank, head=head, slot=slot, n0=n0: e.activation(
                                pt[:, slot, n0:512], bank[:, n0:512], AF.Exp,
                                bias=negm[:, head:head + 1], scale=1.0), "s_s")
                            p.add("scalar", fn, waits=wS)
                        if j >= 0:
                            for which in range(2):
                                ht = 2 * tj + which
                                slot = ht % 4
                                fn, msk_cnt[ht] = mk(lambda e, slot=slot, n0=n0: e.tensor_tensor(
                                    pt[:, slot, n0:n0 + 128], pt[:, slot, n0:n0 + 128],
                                    mtri[:, :], OP.mult), "s_v")
                                mw = W(("s_s", exp_cnt[ht]))
                                if tj == 0:
                                    mw += W(("d_gcst", gsetup))
                                p.add("vector", fn, waits=mw)
                        pv_list = []
                        if kb >= 1:
                            pv_list.append(kb - 1)
                        if kb == nkb - 1:
                            pv_list.append(kb)
                        for kpv in pv_list:
                            tjp, n0p = sc_of[kpv]
                            jp = kpv - 4 * qt
                            for which, (head, pvbank) in enumerate(((hA, pvA), (hB, pvB))):
                                htp = 2 * tjp + which
                                slot = htp % 4
                                wP = []
                                if jp >= 0:
                                    wP += W(("s_v", msk_cnt[htp]))
                                else:
                                    wP += W(("s_s", exp_cnt[htp]))
                                if kpv == 0 and ghA - 4 + which >= 0:
                                    wP += W(("s_v", yev_cnt.get(ghA - 4 + which, 0)))
                                fn, pv_cnt[htp] = mk(lambda e, head=head, pvbank=pvbank, kpv=kpv,
                                                     slot=slot, n0p=n0p, nkb=nkb: e.matmul(
                                    pvbank[:, n0p:512],
                                    v_sb[:, kpv, 65 * head:65 * head + 65],
                                    pt[:, slot, n0p:512],
                                    start=(kpv == 0), stop=(kpv == nkb - 1)), "s_t")
                                p.add("tensor", fn, waits=wP)
                        tj += 1
                    lastA = 2 * (tj - 1)
                    for which, (pvbank,) in enumerate(((pvA,), (pvB,))):
                        gh = ghA + which
                        head = 2 * hp + which
                        ysl = gh % 4
                        wv_ = W(("s_t", pv_cnt[lastA + which]))
                        if yT_uses[ysl] > 0:
                            wv_ += W((f"d_y{ysl}", 16 * yT_uses[ysl]))
                        fn, yev_cnt[gh] = mk(lambda e, ysl=ysl, pvbank=pvbank: e.tensor_copy(
                            ystage[0:65, ysl, :], pvbank[0:65, :]), "s_v")
                        p.add("vector", fn, waits=wv_)
                        wt2 = W(("s_v", yev_cnt[gh]))
                        if qt > 0:
                            wt2 += W(("s_v", acc_cnt[qt - 1]))
                        fn, _ = mk(lambda e, ysl=ysl, head=head: e.dma_start(
                            yqt[:, :, 80 * head:80 * (head + 1)], ystage[:, ysl, :],
                            transpose=True), f"d_y{ysl}", 16)
                        p.add("sync", fn, waits=wt2)
                        yT_uses[ysl] += 1
                wA = []
                for sl4 in range(4):
                    wA += W((f"d_y{sl4}", 16 * yT_uses[sl4]))
                g1 = None
                for u in range(4):
                    fn, g1 = mk(lambda e, u=u: e.reciprocal(
                        rec_sb[:, 16 * u:16 * (u + 1)],
                        bass.AP(yqt, u * NH * 80 + 64, [[4 * NH * 80, 128], [80, 16]])), "s_vg")
                    p.add("vector", fn, waits=wA if u == 0 else ())
                wA2 = W(("s_vg", g1))
                first = True
                for u in range(4):
                    for h in range(NH):
                        fn, c_add = mk(lambda e, qt=qt, u=u, h=h: e.scalar_tensor_tensor(
                            acc[:, 4 * qt + u, 64 * h:64 * (h + 1)],
                            yqt[:, u, 80 * h:80 * h + 64],
                            rec_sb[:, 16 * u + h:16 * u + h + 1],
                            acc[:, 4 * qt + u, 64 * h:64 * (h + 1)],
                            OP.mult, OP.add), "s_v")
                        p.add("vector", fn, waits=wA2 if first else ())
                        first = False
                acc_cnt[qt] = c_add
                for u in range(4):
                    m = 4 * qt + u
                    sl = m % 2
                    w = []
                    if m >= 2:
                        w += W((f"d_ob{sl}", 16 * (m // 2)))
                    fn, xf_done = mk(lambda e, m=m, sl=sl: e.dma_start(
                        xf[:, sl, :], bass.AP(x_d, m * 128 * C, [[C, 128], [1, C]])),
                        f"d_xf{sl}", 16)
                    p.add("gpsimd", fn, waits=w)
                    wv_ = W((f"d_xf{sl}", xf_done), ("s_v", acc_cnt[qt]))
                    fn, add_cnt[m] = mk(lambda e, m=m, sl=sl: e.tensor_add(
                        xf[:, sl, :], xf[:, sl, :], acc[:, m, :]), "s_g")
                    p.add("gpsimd", fn, waits=wv_)
                    fn, _ = mk(lambda e, m=m, sl=sl: e.dma_start(
                        bass.AP(out_d, m * 128 * C, [[C, 128], [1, C]]), xf[:, sl, :]),
                        f"d_ob{sl}", 16)
                    p.add("gpsimd", fn, waits=W(("s_g", add_cnt[m])))
            p.ops["gpsimd"].append(
                (W(("d_ob0", CNT["d_ob0"]), ("d_ob1", CNT["d_ob1"])), lambda e: e.nop()))
            emit_block(nc, p, "b5_attn")
        es_qkv.close()


    return nc, dbg_d


_NC_CACHE = {}


def get_nc():
    key = DEBUG_PHASE
    if key not in _NC_CACHE:
        _NC_CACHE[key] = build_nc()
    return _NC_CACHE[key]


def kernel(L0_attn_x, L0_attn_W, L0_attn_b, L0_ffn_w1, L0_ffn_b1, L0_ffn_w2,
           L0_ffn_b2, L0_ffn_centroids, L0_ffn_w_route):
    x = np.ascontiguousarray(np.asarray(L0_attn_x, dtype=np.float32))
    Wm = np.asarray(L0_attn_W, dtype=np.float32)
    b = np.asarray(L0_attn_b, dtype=np.float32)
    w1 = np.ascontiguousarray(np.asarray(L0_ffn_w1, dtype=np.float32))
    b1 = np.asarray(L0_ffn_b1, dtype=np.float32)
    w2 = np.ascontiguousarray(np.asarray(L0_ffn_w2, dtype=np.float32))
    b2 = np.ascontiguousarray(np.asarray(L0_ffn_b2, dtype=np.float32))
    cent = np.ascontiguousarray(np.asarray(L0_ffn_centroids, dtype=np.float32))
    wr = np.asarray(L0_ffn_w_route, dtype=np.float32)

    wt = np.ascontiguousarray(Wm.T)
    bqk = np.ascontiguousarray(b[0:2048].reshape(16, 128).T)
    bv = np.ascontiguousarray(b[2048:3072].reshape(1, 1024))
    b1c = np.ascontiguousarray(b1.reshape(8, 4, 128).transpose(2, 0, 1))
    wrt = np.ascontiguousarray(wr.T)

    nc, _ = get_nc()
    shared = {"wt": wt, "bqk": bqk, "bv": bv, "w1": w1, "b1": b1c,
              "w2": w2, "b2": b2, "cent": cent, "wrt": wrt}
    in_maps = [dict(shared, x=np.ascontiguousarray(x[i])) for i in range(8)]
    res = run_bass_kernel_spmd(nc, in_maps, core_ids=list(range(8)))
    out = np.stack([r["out"] for r in res.results], axis=0)
    return out
